# version 23
# baseline (speedup 1.0000x reference)
import numpy as np
import ml_dtypes
import concourse.bass as bass
import concourse.mybir as mybir
from concourse.bass_utils import run_bass_kernel_spmd
from contextlib import ExitStack

F32 = mybir.dt.float32
BF16 = mybir.dt.bfloat16
AF = mybir.ActivationFunctionType
ALU = mybir.AluOpType
AX = mybir.AxisListType.X

N = 4096
DIAG_FFN_FENCE = False
D = 1024
H = 16
HS = 64
CB = 8
NPAIR = 8
WIN = 2176
WOFF = 1920
LAT = 256
EPS = 1e-5
LN8 = float(np.log(8.0))


class Sem:
    def __init__(self, handle, name):
        self.h = handle
        self.name = name
        self.n = 0


class Sched:
    ENGINES = ("sync", "gpsimd", "tensor", "scalar", "vector")

    def __init__(self):
        self.ops = {e: [] for e in self.ENGINES}
        self.pending_fence = {e: [] for e in self.ENGINES}

    def fence(self, semvals):
        for e in self.ENGINES:
            self.pending_fence[e] = list(semvals)

    def op(self, eng, fn, waits=(), inc=None, inck=1):
        w = self.pending_fence[eng] + list(waits)
        self.pending_fence[eng] = []
        if inc is not None:
            inc.n += inck
        self.ops[eng].append((fn, w, inc, inck))
        return inc.n if inc is not None else None

    def emit(self, block):
        for eng_name in self.ENGINES:
            ops = self.ops[eng_name]
            if not ops:
                continue
            section = getattr(block, eng_name)

            def body(eng, ops=ops):
                for fn, waits, inc, inck in ops:
                    best = {}
                    for (sem, val) in waits:
                        if val > 0:
                            prev = best.get(sem.name)
                            best[sem.name] = (sem, max(val, prev[1]) if prev else val)
                    for sem, val in best.values():
                        eng.wait_ge(sem.h, val)
                    instr = fn(eng)
                    if inc is not None:
                        assert instr is not None
                        instr.then_inc(inc.h, inck)

            section(body)


def ceil_div(a, b):
    return (a + b - 1) // b


def build_program(debug_taps=()):
    nc = bass.Bass("TRN2", target_bir_lowering=False)

    def inp(name, shape):
        return nc.dram_tensor(name, shape, F32, kind="ExternalInput").ap()

    x0_cm = inp("x0_cm", [D, N])
    wq_a = inp("wq_a", [D, D]); wk_a = inp("wk_a", [D, D]); wv_a = inp("wv_a", [D, D])
    bq_a = inp("bq_a", [128, CB]); bk_a = inp("bk_a", [128, CB])
    bv_a_row = inp("bv_a_row", [1, D])
    wo_a = inp("wo_a", [D, D]); bo_a = inp("bo_a", [128, CB])
    wr_f = inp("wr_f", [D, D]); wkf_f = inp("wkf_f", [D, D]); wvf_f = inp("wvf_f", [D, D])
    mu_r = inp("mu_r", [128, CB]); mu_k = inp("mu_k", [128, CB])
    wq_b = inp("wq_b", [D, D]); bq_b = inp("bq_b", [128, CB])
    wl_b = inp("wl_b", [D, LAT]); bl_b = inp("bl_b", [128, 2])
    wk_b = inp("wk_b", [LAT, D]); bk_b = inp("bk_b", [128, CB])
    wv_b = inp("wv_b", [LAT, D]); bv_b_row = inp("bv_b_row", [1, D])
    wo_b = inp("wo_b", [D, D]); bo_b = inp("bo_b", [128, CB])

    tri_np = (np.arange(128)[:, None] <= np.arange(128)[None, :]).astype(np.float32)
    tri_c = nc.inline_tensor(tri_np, name="tri_c")
    sel_c = nc.inline_tensor(np.array([[0.0], [1.0]], np.float32), name="sel_c")

    out_cm = nc.dram_tensor("out_cm", [D, WIN], F32, kind="ExternalOutput").ap()

    qa_h = nc.dram_tensor("qa_h", [NPAIR, 128, N], BF16)
    ka_h = nc.dram_tensor("ka_h", [NPAIR, 128, N], BF16)
    va_h = nc.dram_tensor("va_h", [N, D], BF16)
    qb_h = nc.dram_tensor("qb_h", [NPAIR, 128, N], BF16)
    kb_h = nc.dram_tensor("kb_h", [NPAIR, 128, N], BF16)
    vb_h = nc.dram_tensor("vb_h", [N, D], BF16)
    oh_h = nc.dram_tensor("oh_h", [NPAIR, 128, N], BF16)
    rfull_h = nc.dram_tensor("rfull_h", [D, N], F32)
    qwin_h = nc.dram_tensor("qwin_h", [NPAIR, 128, WIN], BF16)
    x2win_h = nc.dram_tensor("x2win_h", [D, WIN], F32)
    x3_h = nc.dram_tensor("x3_h", [D, WIN], F32)
    wr_hb = nc.dram_tensor("wr_hb", [CB, 128, D], BF16)
    wkf_hb = nc.dram_tensor("wkf_hb", [CB, 128, D], BF16)
    wvf_hb = nc.dram_tensor("wvf_hb", [CB, 128, D], BF16)

    dbg = {}
    for name in debug_taps:
        t = {"qa": qa_h, "ka": ka_h, "va": va_h, "qb": qb_h, "kb": kb_h,
             "vb": vb_h, "oh": oh_h, "rfull": rfull_h, "x3": x3_h,
             "oh2": oh_h, "x1": None, "ccols": None, "nmax": None,
             "fr": None, "fka": None, "fxr": None, "fxk": None,
             "fxn": None, "x3t": None, "sc16": None}[name]
        if t is not None:
            dbg[name] = (t, nc.dram_tensor("dbg_" + name, list(t.shape), t.dtype,
                                           kind="ExternalOutput").ap())
    dbg_sb = {}
    for name, shape, dt in (("x1", [D, N], F32), ("ccols", [128, 18], F32),
                            ("nmax", [2, 32], F32), ("fr", [128, CB * 256], F32),
                            ("fka", [128, CB * 256], BF16), ("fxr", [128, CB * 256], BF16),
                            ("fxk", [128, CB * 256], BF16), ("fxn", [128, CB * 257], BF16),
                            ("x3t", [128, CB * 257], F32), ("sc16", [128, N], F32)):
        if name in debug_taps:
            dbg_sb[name] = nc.dram_tensor("dbg_" + name, shape, dt,
                                          kind="ExternalOutput").ap()

    with ExitStack() as ctx:
        SBUF_TOP = 229344
        cursor = {"A": 16384, "B": None}
        B_BASE = [0]

        OFFS = {}

        def _alloc(name, shape, dt, region):
            nbytes = int(np.prod(shape[1:])) * mybir.dt.size(dt)
            off = cursor[region]
            off = (off + 31) // 32 * 32
            cursor[region] = off + nbytes
            assert cursor[region] <= SBUF_TOP, (name, cursor[region])
            OFFS[name] = off
            return nc.alloc_sbuf_tensor_at(name, list(shape), dt, offset=off)

        def sbuf(name, shape, dt):
            return _alloc(name, shape, dt, "A")

        def sbufB(name, shape, dt):
            return _alloc(name, shape, dt, "B")

        def resetB():
            cursor["B"] = B_BASE[0]

        def mksem(name):
            return Sem(ctx.enter_context(nc.semaphore(name)), name)

        R = sbuf("R", [128, CB * N], F32)
        SC16 = sbuf("SC16", [128, N], F32)
        ones_st = sbuf("ones_st", [128, 128], BF16)
        trimask = sbuf("trimask", [128, 128], BF16)
        e2 = sbuf("e2", [128, 2], BF16)
        biases = sbuf("biases", [128, 24], F32)
        biases2 = sbuf("biases2", [128, 26], F32)
        mus = sbuf("mus", [128, 16], F32)
        brow_a = sbuf("brow_a", [1, D], BF16)
        brow_b = sbuf("brow_b", [1, D], BF16)
        sel_f = sbuf("sel_f", [2, 1], F32)
        ccols = sbuf("ccols", [128, 18], F32)
        onecol = sbuf("onecol", [128, 1], F32)
        nmax = sbuf("nmax", [2, 32], F32)
        cb_t = sbuf("cb_t", [2, 2 * NPAIR], F32)
        cb_ln8 = sbuf("cb_ln8", [2, 1], F32)
        ntmp = sbuf("ntmp", [2, 1], F32)
        nsq = sbuf("nsq", [128, 512], BF16)
        ones_f = sbuf("ones_f", [128, 128], F32)
        xb16 = sbuf("xb16", [128, 512], BF16)
        xsq = sbuf("xsq", [128, 512], BF16)
        xb16b = sbuf("xb16b", [128, 512], BF16)
        xsqb = sbuf("xsqb", [128, 512], BF16)
        B_BASE[0] = (cursor["A"] + 31) // 32 * 32
        resetB()

        class Carve:
            def __init__(self, base):
                self.base = base

            def __getitem__(self, key):
                rows, cols = key
                return SC16[rows, self.base + cols.start: self.base + cols.stop]

        lnw_a = Carve(0)
        lnw_b = Carve(1024)
        lnt = Carve(2048)
        st_negm = Carve(3072)
        st_var = Carve(3584)

        arena = ctx.enter_context(nc.psum_tensor("arena", [128, 4096], F32))

        LD = mksem("LD"); LG = mksem("LG"); PE = mksem("PE")
        SE = mksem("SE"); VE = mksem("VE"); ST = mksem("ST")
        WS = [mksem("WS0"), mksem("WS1"), mksem("WS2"), mksem("WS3")]
        QT = mksem("QT"); KV = mksem("KV"); OS = mksem("OS"); QT2 = mksem("QT2")
        ALLSEMS = [LD, LG, PE, SE, VE, ST, QT, KV, OS, QT2] + WS

        S = Sched()

        def allvals():
            return [(s, s.n) for s in ALLSEMS]

        def ps(c0, w):
            return arena[:, c0:c0 + w]

        G = {}

        def guard(name):
            return list(G.get(name, []))

        def setg(name, *svs):
            G[name] = [sv for sv in svs if sv[1] > 0]

        def vflush():
            S.op("vector", lambda v: v.tensor_copy(ntmp[:, 0:1], cb_ln8[:, 0:1]),
                 inc=VE)
            return VE.n

        dyn = {}
        dyn2 = {}

        def _gp_init(g):
            pid = g.snap(nc.partition_id(), min_val=0, max_val=7)
            dyn2[0] = g.snap((pid % 2) * WOFF, min_val=0, max_val=WOFF)
            return None
        S.op("gpsimd", _gp_init)

        S.op("vector", lambda v: v.memset(ones_st[:], 1.0))
        S.op("vector", lambda v: v.memset(ones_f[:], 1.0))
        S.op("vector", lambda v: v.memset(e2[0:64, 0:1], 1.0))
        S.op("vector", lambda v: v.memset(e2[64:128, 0:1], 0.0))
        S.op("vector", lambda v: v.memset(e2[0:64, 1:2], 0.0))
        S.op("vector", lambda v: v.memset(e2[64:128, 1:2], 1.0))
        S.op("vector", lambda v: v.memset(ccols[:, 16:17], EPS))
        S.op("vector", lambda v: v.memset(ccols[:, 17:18], 0.0))
        S.op("vector", lambda v: v.memset(onecol[:], 1.0))
        S.op("vector", lambda v: v.memset(cb_ln8[:], -LN8))
        S.op("vector", lambda v: v.memset(nmax[:], 0.0), inc=VE)

        S.op("gpsimd", lambda g: g.dma_start(out=trimask[:], in_=tri_c.ap()), inc=LG, inck=16)
        S.op("gpsimd", lambda g: g.dma_start(out=brow_a[:], in_=bv_a_row), inc=LG, inck=16)
        S.op("gpsimd", lambda g: g.dma_start(out=brow_b[:], in_=bv_b_row), inc=LG, inck=16)
        S.op("gpsimd", lambda g: g.dma_start(out=sel_f[:], in_=sel_c.ap()), inc=LG, inck=16)
        for dst, src in ((biases[:, 0:8], bq_a), (biases[:, 8:16], bk_a),
                         (biases[:, 16:24], bo_a), (biases2[:, 0:8], bq_b),
                         (biases2[:, 8:16], bk_b), (biases2[:, 16:24], bo_b),
                         (biases2[:, 24:26], bl_b), (mus[:, 0:8], mu_r),
                         (mus[:, 8:16], mu_k)):
            S.op("sync", lambda s, dst=dst, src=src: s.dma_start(out=dst, in_=src),
                 inc=LD, inck=16)

        S.op("sync", lambda s: s.dma_start(
            out=R.ap().rearrange("p (cb t) -> p cb t", cb=CB),
            in_=x0_cm.rearrange("(cb p) t -> p cb t", p=128)), inc=LD, inck=16)

        wbigQ = sbufB("wbigQ", [128, CB * D], BF16)
        wbigK = sbufB("wbigK", [128, CB * D], BF16)
        wvh = sbufB("wvh", [128, CB * 512], BF16)
        xn1 = sbufB("xn1", [128, CB * 512], BF16)
        evac1 = sbufB("evac1", [128, 4 * 512], BF16)
        stash_g = []
        for (wsrc, wdst) in ((wr_f, wr_hb), (wkf_f, wkf_hb), (wvf_f, wvf_hb)):
            S.op("gpsimd", lambda g, wsrc=wsrc: g.dma_start(
                out=wbigQ.ap().rearrange("p (cb o) -> p cb o", cb=CB),
                in_=wsrc.rearrange("(cb p) o -> p cb o", p=128)),
                waits=list(stash_g), inc=LG, inck=16)
            ldv = LG.n

            def _store(s, wdst=wdst):
                for ocb in range(CB):
                    s.dma_start(
                        out=wdst.ap()[ocb].rearrange("p (cb j) -> p cb j", cb=CB),
                        in_=wbigQ.ap().rearrange("p (cb o) -> p cb o", cb=CB)
                        [:, :, ocb * 128:(ocb + 1) * 128]).then_inc(LD.h, 16)
                return None
            S.op("sync", _store, waits=[(LG, ldv)])
            LD.n += 16 * CB
            stash_g = [(LD, LD.n)]

        S.fence(allvals())

        def ln_stats_apply(tw, src_fn, dst_fn, src_waits=(), apply_waits=()):
            nsl = ceil_div(tw, 512)
            for si in range(nsl):
                c0 = si * 512
                w = min(512, tw - c0)
                for cb in range(CB):
                    xb = xb16 if cb % 2 == 0 else xb16b
                    xs = xsq if cb % 2 == 0 else xsqb
                    wts = (guard(f"xb16_{cb % 2}")
                           + (list(src_waits) if (cb == 0 and si == 0) else []))
                    S.op("vector", lambda v, cb=cb, c0=c0, w=w, xb=xb: v.tensor_copy(
                        xb[:, 0:w], src_fn(cb)[:, c0:c0 + w]), waits=wts, inc=VE)
                    vcp = VE.n
                    S.op("vector", lambda v, cb=cb, w=w, xb=xb, xs=xs: v.tensor_mul(
                        xs[:, 0:w], xb[:, 0:w], xb[:, 0:w]),
                        waits=[(VE, vcp)], inc=VE)
                    vcb = VE.n

                    def _mmcb(t, cb=cb, w=w, xb=xb, xs=xs):
                        t.matmul(ps(3072, w), ones_st[:], xb[:, 0:w],
                                 start=(cb == 0), stop=(cb == CB - 1))
                        return t.matmul(ps(3584, w), ones_st[:], xs[:, 0:w],
                                        start=(cb == 0), stop=(cb == CB - 1))
                    S.op("tensor", _mmcb,
                         waits=[(VE, vcb)] + (guard("stats_ps") if cb == 0 else []),
                         inc=PE)
                    setg(f"xb16_{cb % 2}", (PE, PE.n))
                pev = PE.n
                S.op("vector", lambda v, w=w: v.tensor_scalar(
                    st_negm[:, 0:w], ps(3072, w), -1.0 / D, None, ALU.mult),
                    waits=[(PE, pev)] + guard("st"), inc=VE)
                S.op("vector", lambda v, w=w: v.tensor_scalar(
                    st_var[:, 0:w], ps(3584, w), 1.0 / D, None, ALU.mult),
                    waits=[(VE, VE.n)], inc=VE)
                S.op("vector", lambda v, w=w: v.tensor_mul(
                    lnt[:, 0:w], st_negm[:, 0:w], st_negm[:, 0:w]),
                    waits=[(VE, VE.n)], inc=VE)
                S.op("vector", lambda v, w=w: v.tensor_sub(
                    st_var[:, 0:w], st_var[:, 0:w], lnt[:, 0:w]),
                    waits=[(VE, VE.n)], inc=VE)
                vstat = VE.n
                setg("stats_ps", (VE, vstat))
                S.op("scalar", lambda sc, w=w, c0=c0: sc.activation(
                    lnw_b[:, c0:c0 + w], st_var[:, 0:w], AF.Ln, bias=ccols[:, 16:17],
                    scale=1.0), waits=[(VE, vstat)] + guard("lnw"))
                S.op("scalar", lambda sc, w=w, c0=c0: sc.activation(
                    lnw_a[:, c0:c0 + w], lnw_b[:, c0:c0 + w], AF.Exp, bias=ccols[:, 17:18],
                    scale=-0.5), inc=SE)
                sev = SE.n
                S.op("vector", lambda v, w=w, c0=c0: v.tensor_mul(
                    lnw_b[:, c0:c0 + w], st_negm[:, 0:w], lnw_a[:, c0:c0 + w]),
                    waits=[(SE, sev)], inc=VE)
                setg("st", (VE, VE.n), (SE, sev))
            for cb in range(CB):
                wts = list(apply_waits) if cb == 0 else []
                S.op("vector", lambda v, cb=cb: v.tensor_mul(
                    lnt[:, 0:tw], src_fn(cb), lnw_a[:, 0:tw]),
                    waits=wts + [(VE, VE.n)], inc=VE)
                S.op("vector", lambda v, cb=cb: v.tensor_add(
                    dst_fn(cb), lnt[:, 0:tw], lnw_b[:, 0:tw]),
                    waits=[(VE, VE.n)], inc=VE)
            setg("lnw", (VE, VE.n))
            return vflush()

        def norm_track(tile_ap, w, col, slot_key, pa2):
            S.op("vector", lambda v: v.tensor_mul(nsq[:, 0:w], tile_ap, tile_ap),
                 waits=guard("nsq"), inc=VE)
            vv = VE.n
            S.op("tensor", lambda t: t.matmul(pa2[0:2, 0:w], e2[:], nsq[:, 0:w],
                                              start=True, stop=True),
                 waits=[(VE, vv)], inc=PE)
            pv = PE.n
            setg("nsq", (PE, pv))
            S.op("vector", lambda v: v.reduce_max(ntmp[:, 0:1], pa2[0:2, 0:w], axis=AX),
                 waits=[(PE, pv)])
            S.op("vector", lambda v: v.tensor_max(nmax[:, col:col + 1],
                                                  nmax[:, col:col + 1], ntmp[0:2, 0:1]),
                 inc=VE)
            setg(slot_key, (VE, VE.n))

        def emit_cbound(qcol0, kcol0, outcol0, psw):
            vpre = VE.n
            S.op("scalar", lambda sc: sc.activation(
                cb_t[0:2, 0:NPAIR], nmax[0:2, qcol0:qcol0 + NPAIR], AF.Ln,
                bias=ccols[0:2, 17:18], scale=1.0), waits=[(VE, vpre)])
            S.op("scalar", lambda sc: sc.activation(
                cb_t[0:2, NPAIR:2 * NPAIR], nmax[0:2, kcol0:kcol0 + NPAIR], AF.Ln,
                bias=ccols[0:2, 17:18], scale=1.0), inc=SE)
            sv = SE.n
            S.op("vector", lambda v: v.tensor_add(
                cb_t[0:2, 0:NPAIR], cb_t[0:2, 0:NPAIR], cb_t[0:2, NPAIR:2 * NPAIR]),
                waits=[(SE, sv)], inc=VE)
            vv = VE.n
            S.op("scalar", lambda sc: sc.activation(
                cb_t[0:2, 0:NPAIR], cb_t[0:2, 0:NPAIR], AF.Exp,
                bias=cb_ln8[0:2, 0:1], scale=0.51), waits=[(VE, vv)], inc=SE)
            sv = SE.n
            S.op("tensor", lambda t: t.matmul(arena[0:1, 2048:2048 + NPAIR],
                                              sel_f[0:2, 0:1], cb_t[0:2, 0:NPAIR],
                                              start=True, stop=True),
                 waits=[(SE, sv)] + list(psw), inc=PE)
            pv = PE.n
            S.op("vector", lambda v: v.tensor_max(
                cb_t[0:1, 0:NPAIR], cb_t[0:1, 0:NPAIR], arena[0:1, 2048:2048 + NPAIR]),
                waits=[(PE, pv)], inc=VE)
            vv = VE.n

            def _bc(t):
                last = None
                for pr in range(NPAIR):
                    last = t.matmul(arena[:, 2048 + pr:2048 + pr + 1], ones_f[0:1, :],
                                    cb_t[0:1, pr:pr + 1], start=True, stop=True)
                return last
            S.op("tensor", _bc, waits=[(VE, vv)], inc=PE)
            pv = PE.n
            S.op("vector", lambda v: v.tensor_scalar(
                ccols[:, outcol0:outcol0 + NPAIR], arena[:, 2048:2048 + NPAIR],
                -1.0, -0.05, ALU.mult, ALU.add), waits=[(PE, pv)], inc=VE)

        pslot = [0]

        def proj_cm(wsb, w_waits, rhs_fn, rhs_ve, tw, out_fn, ncontr=CB):
            nsl = ceil_div(tw, 512)
            for ocb in range(CB):
                for si in range(nsl):
                    c0 = si * 512
                    w = min(512, tw - c0)
                    slot = pslot[0]
                    pslot[0] = (pslot[0] + 1) % 4
                    pa = ps(slot * 512, w)

                    def _mm(t, ocb=ocb, c0=c0, w=w, pa=pa):
                        last = None
                        for cb in range(ncontr):
                            last = t.matmul(
                                pa, wsb[:, cb * D + ocb * 128: cb * D + ocb * 128 + 128],
                                rhs_fn(cb, c0, w), start=(cb == 0), stop=(cb == ncontr - 1))
                        return last
                    wts = ([(VE, rhs_ve)] if rhs_ve else []) + list(w_waits) + guard(f"slot{slot}")
                    S.op("tensor", _mm, waits=wts, inc=PE)
                    out_fn(ocb, c0, w, pa, PE.n, slot)

        def load_wbig(dst, wsrc, waits=()):
            S.op("gpsimd", lambda g: g.dma_start(
                out=dst.ap().rearrange("p (cb o) -> p cb o", cb=CB),
                in_=wsrc.rearrange("(cb p) o -> p cb o", p=128)),
                waits=list(waits), inc=LG, inck=16)
            return LG.n

        wq_ld = load_wbig(wbigQ, wq_a, waits=stash_g)
        wk_ld = load_wbig(wbigK, wk_a)

        for tt in range(8):
            t0 = tt * 512
            ve_xn = ln_stats_apply(
                512,
                lambda cb, t0=t0: R[:, cb * N + t0: cb * N + t0 + 512],
                lambda cb: xn1[:, cb * 512:(cb + 1) * 512],
                apply_waits=guard("xn"))

            for (wsb, wld, hb, bcol0, ncol0) in ((wbigQ, wq_ld, qa_h, 0, 0),
                                                 (wbigK, wk_ld, ka_h, 8, 8)):
                def out_qk(ocb, c0, w, pa, pev, slot, hb=hb, bcol0=bcol0,
                           ncol0=ncol0, t0=t0):
                    ev = evac1[:, slot * 512: slot * 512 + w]
                    S.op("vector", lambda v: v.tensor_scalar(
                        ev, pa, biases[:, bcol0 + ocb:bcol0 + ocb + 1], None, ALU.add),
                        waits=[(PE, pev)] + guard(f"evac{slot}"), inc=VE)
                    vv = VE.n
                    S.op("sync", lambda s: s.dma_start(
                        out=hb.ap()[ocb][:, t0 + c0:t0 + c0 + w], in_=ev),
                        waits=[(VE, vv)], inc=LD, inck=16)
                    setg(f"evac{slot}", (LD, LD.n))
                    norm_track(ev, w, ncol0 + ocb, f"slot{slot}", pa)
                proj_cm(wsb, [(LG, wld)],
                        lambda cb, c0, w: xn1[:, cb * 512 + c0: cb * 512 + c0 + w],
                        ve_xn, 512, out_qk)

            for vc in range(2):
                S.op("gpsimd", lambda g, vc=vc: g.dma_start(
                    out=wvh.ap().rearrange("p (cb o) -> p cb o", cb=CB),
                    in_=wv_a[:, vc * 512:(vc + 1) * 512]
                    .rearrange("(cb p) o -> p cb o", p=128)),
                    waits=guard("wvh"), inc=LG, inck=16)
                wvl = LG.n
                for tb in range(4):
                    tof = t0 + tb * 128
                    pa = ps(2048 + (tb % 2) * 512, 512)

                    def _mmv(t, tb=tb, vc=vc, pa=pa):
                        last = None
                        for cb in range(CB):
                            last = t.matmul(
                                pa, xn1[:, cb * 512 + tb * 128: cb * 512 + tb * 128 + 128],
                                wvh[:, cb * 512: cb * 512 + 512],
                                start=(cb == 0), stop=False)
                        return t.matmul(pa, ones_st[0:1, :],
                                        brow_a[0:1, vc * 512:vc * 512 + 512],
                                        start=False, stop=True)
                    S.op("tensor", _mmv,
                         waits=[(VE, ve_xn), (LG, wvl)] + guard(f"vps{tb % 2}"), inc=PE)
                    pev = PE.n
                    setg("wvh", (PE, pev))
                    S.op("vector", lambda v, tb=tb, pa=pa: v.tensor_copy(
                        evac1[:, (tb % 2) * 512:(tb % 2) * 512 + 512], pa),
                        waits=[(PE, pev)] + guard(f"evac{tb % 2}"), inc=VE)
                    setg(f"vps{tb % 2}", (VE, VE.n))
                    vv = VE.n
                    S.op("sync", lambda s, tof=tof, tb=tb, vc=vc: s.dma_start(
                        out=va_h.ap()[tof:tof + 128, vc * 512:(vc + 1) * 512],
                        in_=evac1[:, (tb % 2) * 512:(tb % 2) * 512 + 512]),
                        waits=[(VE, vv)], inc=LD, inck=16)
                    setg(f"evac{tb % 2}", (LD, LD.n))
            setg("xn", (PE, PE.n))

        emit_cbound(0, 8, 0, guard("vps0") + guard("vps1"))
        attn_gate = [VE.n]
        S.fence(allvals())

        def attn_phase(q_hbm, k_hbm, v_hbm, ccol, causal, qwidths, dynwin,
                       kpb, vpb, qtl, expS, o_un, rcp, qtl2):
            qtot = sum(w for (_, w) in qwidths)

            def _emit_pvh(kt, c0, qw, pv_w, slot, half, first):
                def _pv(t, kt=kt, c0=c0, qw=qw, first=first, slot=slot, half=half):
                    last = None
                    cc = c0
                    while cc < qw:
                        w = min((512 - (cc % 512)) if (cc % 512) else 512, qw - cc)
                        last = t.matmul(
                            arena[:, 2048 + half * 1024 + cc:
                                  2048 + half * 1024 + cc + w],
                            vpb[:, kt * 256 + half * 128: kt * 256 + half * 128 + 128],
                            expS[:, slot * 1024 + cc: slot * 1024 + cc + w],
                            start=first, stop=False)
                        cc += w
                    return last
                if first:
                    pv_w = pv_w + guard("pv_ps")
                S.op("tensor", _pv, waits=pv_w, inc=PE)
                setg(f"expS{slot}", (PE, PE.n))
            S.op("vector", lambda v: v.memset(
                vpb.ap().rearrange("p (tb q) -> p tb q", tb=32)[:, :, 64:192], 1.0),
                inc=VE)
            setg("vpb", (VE, VE.n))
            for pr in range(NPAIR):
                S.op("sync", lambda s, pr=pr: s.dma_start(
                    out=kpb[:], in_=k_hbm.ap()[pr]),
                    waits=guard("kpb"), inc=KV, inck=16)
                S.op("gpsimd", lambda g, pr=pr: g.dma_start(
                    out=vpb.ap().rearrange("p (tb q) -> p tb q", tb=32)[:, :, 0:64],
                    in_=v_hbm.ap()[:, pr * 128:pr * 128 + 64]
                    .rearrange("(tb p) c -> p tb c", p=128)),
                    waits=guard("vpb"), inc=KV, inck=16)
                S.op("gpsimd", lambda g, pr=pr: g.dma_start(
                    out=vpb.ap().rearrange("p (tb q) -> p tb q", tb=32)[:, :, 192:256],
                    in_=v_hbm.ap()[:, pr * 128 + 64:pr * 128 + 128]
                    .rearrange("(tb p) c -> p tb c", p=128)), inc=KV, inck=16)
                kvld = KV.n

                ve_collect = VE.n
                qbufs = [qtl, qtl2]
                qsems = [QT, QT2]
                qloadv = {}

                def _q_load(ti):
                    if ti >= len(qwidths) or ti in qloadv:
                        return
                    qloc, qw = qwidths[ti]
                    b = ti % 2
                    S.op("sync", lambda s, pr=pr, qloc=qloc, qw=qw, b=b: s.dma_start(
                        out=qbufs[b][:, 0:qw], in_=q_hbm.ap()[pr][:, qloc:qloc + qw]),
                        waits=guard(f"qtl{b}"), inc=qsems[b], inck=16)
                    qloadv[ti] = qsems[b].n

                for ti, (qloc, qw) in enumerate(qwidths):
                    _q_load(ti)
                    _q_load(ti + 1)
                    qtlb = qbufs[ti % 2]
                    qld = qloadv[ti]
                    qsem = qsems[ti % 2]

                    nkt = 32 if not causal else (qloc + qw) // 128
                    pend = None
                    for i in range(nkt):
                        kt = i
                        k0 = kt * 128
                        diag = causal and (k0 >= qloc)
                        c0 = (k0 - qloc) if diag else 0
                        pvs = []
                        for half in range(2):
                            slot = (2 * i + half) % 4

                            def _smmh(t, kt=kt, c0=c0, qw=qw, half=half, qtlb=qtlb):
                                b0 = half * 64
                                tp = (0, 0) if half == 0 else (64, 0)
                                last = None
                                cc = c0
                                while cc < qw:
                                    w = min((512 - (cc % 512)) if (cc % 512) else 512,
                                            qw - cc)
                                    last = t.matmul(
                                        arena[:, half * 1024 + cc: half * 1024 + cc + w],
                                        kpb[b0:b0 + 64, kt * 128:(kt + 1) * 128],
                                        qtlb[b0:b0 + 64, cc:cc + w],
                                        start=True, stop=True, tile_position=tp)
                                    cc += w
                                return last
                            sw = [(KV, kvld), (qsem, qld)] + guard(f"score{half}")
                            if i == 0 and half == 0:
                                sw += ([(VE, attn_gate[0])] + guard("pv_ps")
                                       + guard("score0") + guard("score1"))
                            S.op("tensor", _smmh, waits=sw, inc=PE)
                            pvs.append([kt, c0, qw, PE.n, slot, half])

                        for half in range(2):
                            kt_, c0_, qw_, pev, slot, _h = pvs[half]
                            S.op("scalar",
                                 lambda sc, pr=pr, c0=c0, qw=qw, slot=slot, half=half:
                                 sc.activation(
                                     expS[:, slot * 1024 + c0: slot * 1024 + qw],
                                     arena[:, half * 1024 + c0: half * 1024 + qw],
                                     AF.Exp, bias=ccols[:, ccol + pr:ccol + pr + 1],
                                     scale=0.125),
                                 waits=[(PE, pev)] + guard(f"expS{slot}"), inc=SE)
                            sev = SE.n
                            setg(f"score{half}", (SE, sev))
                            pv_w = [(SE, sev)]
                            if diag:
                                S.op("vector", lambda v, c0=c0, slot=slot: v.tensor_mul(
                                    expS[:, slot * 1024 + c0: slot * 1024 + c0 + 128],
                                    expS[:, slot * 1024 + c0: slot * 1024 + c0 + 128],
                                    trimask[:]), waits=[(SE, sev)], inc=VE)
                                pv_w.append((VE, VE.n))
                            pvs[half][3] = pv_w

                        if i == 0:
                            for (kt_, c0_, qw_, pv_w, slot, half) in pvs:
                                _emit_pvh(kt_, c0_, qw_, pv_w, slot, half, True)
                            pend = None
                        else:
                            if pend is not None:
                                for (kt_, c0_, qw_, pv_w, slot, half) in pend:
                                    _emit_pvh(kt_, c0_, qw_, pv_w, slot, half, False)
                            pend = pvs
                    if pend is not None:
                        for (kt_, c0_, qw_, pv_w, slot, half) in pend:
                            _emit_pvh(kt_, c0_, qw_, pv_w, slot, half, False)
                    pv_done = PE.n

                    S.op("vector", lambda v, qloc=qloc, qw=qw: v.tensor_copy(
                        SC16[64:128, qloc:qloc + qw], arena[64:128, 2048:2048 + qw]),
                        waits=[(PE, pv_done)] + guard("SC16") + guard("o_un"))
                    S.op("vector", lambda v, qloc=qloc, qw=qw: v.tensor_copy(
                        SC16[0:64, qloc:qloc + qw], arena[0:64, 3072:3072 + qw]))
                    S.op("vector", lambda v, qloc=qloc, qw=qw: v.tensor_copy(
                        o_un[0:64, qloc:qloc + qw], arena[0:64, 2048:2048 + qw]))
                    S.op("vector", lambda v, qloc=qloc, qw=qw: v.tensor_copy(
                        o_un[64:128, qloc:qloc + qw], arena[64:128, 3072:3072 + qw]), inc=VE)
                    ve_collect = vflush()
                    setg("pv_ps", (VE, ve_collect))
                    setg(f"qtl{ti % 2}", (PE, pv_done))

                first = True
                for r0 in range(0, qtot, 2048):
                    rw = min(2048, qtot - r0)
                    S.op("scalar", lambda sc, r0=r0, rw=rw: sc.activation(
                        rcp[:, 0:rw], SC16[:, r0:r0 + rw], AF.Ln, bias=ccols[:, 17:18],
                        scale=1.0), waits=[(VE, ve_collect)] if first else ())
                    first = False
                    S.op("scalar", lambda sc, r0=r0, rw=rw: sc.activation(
                        SC16[:, r0:r0 + rw], rcp[:, 0:rw], AF.Exp, bias=ccols[:, 17:18],
                        scale=-1.0), inc=SE)
                se_norm = SE.n

                def _realign(t, qtot=qtot):
                    last = None
                    for cc in range(0, qtot, 512):
                        w = min(512, qtot - cc)
                        last = t.matmul(arena[0:64, cc:cc + w], ones_f[64:65, 0:64],
                                        SC16[64:65, cc:cc + w], start=True, stop=True)
                        last = t.matmul(arena[64:128, cc:cc + w], ones_f[0:1, 0:64],
                                        SC16[0:1, cc:cc + w], start=True, stop=True)
                    return last
                S.op("tensor", _realign,
                     waits=([(SE, se_norm)] + guard("score0") + guard("score1")
                            + guard("pv_ps")), inc=PE)
                pe_re = PE.n
                S.op("vector", lambda v, qtot=qtot: v.tensor_mul(
                    o_un[:, 0:qtot], o_un[:, 0:qtot], arena[:, 0:qtot]),
                    waits=[(PE, pe_re)], inc=VE)
                ve_norm = vflush()
                S.op("sync", lambda s, pr=pr, qtot=qtot: s.dma_start(
                    out=oh_h.ap()[pr][:, 0:qtot], in_=o_un[:, 0:qtot]),
                    waits=[(VE, ve_norm)], inc=OS, inck=16)
                setg("o_un", (OS, OS.n))
                setg("SC16", (PE, pe_re))
                setg("score0", (VE, ve_norm))
                setg("score1", (VE, ve_norm))
                setg("pv_ps", (VE, ve_norm))
                setg("kpb", (PE, pv_done))
                setg("vpb", (PE, pv_done))

        resetB()
        kpbA = sbufB("kpbA", [128, N], BF16)
        vpbA = sbufB("vpbA", [128, 32 * 256], BF16)
        qtlA = sbufB("qtlA", [128, 1024], BF16)
        qtlA2 = sbufB("qtlA2", [128, 1024], BF16)
        expSA = sbufB("expSA", [128, 2 * 2048], BF16)
        o_unA = sbufB("o_unA", [128, N], BF16)
        rcpA = sbufB("rcpA", [128, 2048], F32)
        attn_phase(qa_h, ka_h, va_h, ccol=0, causal=True,
                   qwidths=[(0, 1024), (1024, 1024), (2048, 1024), (3072, 1024)],
                   dynwin=False,
                   kpb=kpbA, vpb=vpbA, qtl=qtlA, expS=expSA, o_un=o_unA,
                   rcp=rcpA, qtl2=qtlA2)
        S.fence(allvals())
        if "oh2" in dbg:
            S.op("sync", lambda s: s.dma_start(out=dbg["oh2"][1], in_=oh_h.ap()),
                 inc=ST, inck=16)
        if "ccols" in dbg_sb:
            S.op("sync", lambda s: s.dma_start(out=dbg_sb["ccols"], in_=ccols[:]),
                 inc=ST, inck=16)
        if "nmax" in dbg_sb:
            S.op("sync", lambda s: s.dma_start(out=dbg_sb["nmax"], in_=nmax[:]),
                 inc=ST, inck=16)
        S.fence(allvals())

        resetB()
        wbigO = sbufB("wbigO", [128, CB * D], BF16)
        ot3 = sbufB("ot3", [128, CB * 1024], BF16)
        woa_ld = load_wbig(wbigO, wo_a)
        for tt in range(4):
            t0 = tt * 1024
            for pr in range(NPAIR):
                S.op("sync", lambda s, pr=pr, t0=t0: s.dma_start(
                    out=ot3[:, pr * 1024:(pr + 1) * 1024], in_=oh_h.ap()[pr][:, t0:t0 + 1024]),
                    waits=guard("ot") if pr == 0 else [], inc=LD, inck=16)
            old = LD.n

            def out_po(ocb, c0, w, pa, pev, slot, t0=t0):
                S.op("vector", lambda v: v.scalar_tensor_tensor(
                    R[:, ocb * N + t0 + c0: ocb * N + t0 + c0 + w], pa,
                    biases[:, 16 + ocb:16 + ocb + 1],
                    R[:, ocb * N + t0 + c0: ocb * N + t0 + c0 + w],
                    ALU.add, ALU.add), waits=[(PE, pev)], inc=VE)
                setg(f"slot{slot}", (VE, VE.n))
            proj_cm(wbigO, [(LG, woa_ld), (LD, old)],
                    lambda cb, c0, w: ot3[:, cb * 1024 + c0: cb * 1024 + c0 + w],
                    0, 1024, out_po)
            setg("ot", (PE, PE.n))
        S.fence(allvals())
        if "x1" in dbg_sb:
            S.op("sync", lambda s: s.dma_start(
                out=dbg_sb["x1"].rearrange("(cb p) t -> p cb t", p=128),
                in_=R.ap().rearrange("p (cb t) -> p cb t", cb=CB)), inc=ST, inck=16)
            S.fence(allvals())

        def ffn_run(ntok, pre_tile_fn, src_fn, dst_fn, final_fn,
                    fxn, fxr, fxk, fka, fr, fwc, frt,
                    TW=256, NBUF=4, frtw=1024):
            wcg = [[] for _ in range(NBUF)]
            se_r = [0]
            tw_prev = [0]
            ntiles = ceil_div(ntok, TW)

            def frontA(ft):
                t0 = ft * TW
                tw = min(TW, ntok - t0)
                extra = (pre_tile_fn(ft, t0, tw) or []) if pre_tile_fn else []
                if ft == 0:
                    for cb in range(CB):
                        S.op("vector", lambda v, cb=cb: v.memset(
                            fxn[:, cb * (TW + 1):cb * (TW + 1) + 1], 0.0))
                else:
                    for cb in range(CB):
                        S.op("vector", lambda v, cb=cb, tp=tw_prev[0]: v.tensor_copy(
                            fxn[:, cb * (TW + 1):cb * (TW + 1) + 1],
                            fxn[:, cb * (TW + 1) + tp:cb * (TW + 1) + tp + 1]))
                ve_xn = ln_stats_apply(
                    tw,
                    lambda cb, t0=t0, tw=tw: src_fn(cb, t0, tw),
                    lambda cb, tw=tw: fxn[:, cb * (TW + 1) + 1: cb * (TW + 1) + 1 + tw],
                    src_waits=extra, apply_waits=guard("fx"))
                tw_prev[0] = tw
                return ve_xn

            def frontB(ft, ve_xn):
                t0 = ft * TW
                tw = min(TW, ntok - t0)
                for cb in range(CB):
                    S.op("vector", lambda v, cb=cb, tw=tw: v.tensor_sub(
                        lnt[:, 0:tw], fxn[:, cb * (TW + 1) + 1: cb * (TW + 1) + 1 + tw],
                        fxn[:, cb * (TW + 1): cb * (TW + 1) + tw]),
                        waits=([(VE, ve_xn)] + guard("fx") + guard("fka")) if cb == 0 else [])
                    S.op("vector", lambda v, cb=cb, tw=tw: v.scalar_tensor_tensor(
                        fxr[:, cb * TW: cb * TW + tw], lnt[:, 0:tw],
                        mus[:, cb:cb + 1], fxn[:, cb * (TW + 1): cb * (TW + 1) + tw],
                        ALU.mult, ALU.add))
                    S.op("vector", lambda v, cb=cb, tw=tw: v.scalar_tensor_tensor(
                        fxk[:, cb * TW: cb * TW + tw], lnt[:, 0:tw],
                        mus[:, 8 + cb:8 + cb + 1], fxn[:, cb * (TW + 1): cb * (TW + 1) + tw],
                        ALU.mult, ALU.add), inc=VE)
                return vflush()

            def rk(ft, ve_mix, kinds):
                t0 = ft * TW
                tw = min(TW, ntok - t0)
                for (wh, xsb, kind) in kinds:
                    for ocb in range(CB):
                        buf = (ocb + (0 if kind == "r" else CB)) % NBUF
                        S.op("sync" if buf % 2 == 0 else "gpsimd",
                             lambda s, wh=wh, ocb=ocb, buf=buf: s.dma_start(
                                 out=fwc[:, buf * D:(buf + 1) * D], in_=wh.ap()[ocb]),
                             waits=wcg[buf], inc=WS[buf], inck=16)
                        wld = WS[buf].n
                        slot = ocb % 4
                        pa = ps(slot * 512, tw)
                        gw = [(VE, ve_mix), (WS[buf], wld)] + guard(f"slot{slot}")

                        def _mmf(t, xsb=xsb, buf=buf, tw=tw, pa=pa):
                            last = None
                            for cbb in range(CB):
                                last = t.matmul(
                                    pa, fwc[:, buf * D + cbb * 128: buf * D + cbb * 128 + 128],
                                    xsb[:, cbb * TW: cbb * TW + tw],
                                    start=(cbb == 0), stop=(cbb == CB - 1))
                            return last
                        S.op("tensor", _mmf, waits=gw, inc=PE)
                        wcg[buf] = [(PE, PE.n)]
                        pev = PE.n
                        if kind == "r":
                            S.op("scalar", lambda sc, ocb=ocb, tw=tw, pa=pa: sc.activation(
                                fr[:, ocb * TW: ocb * TW + tw], pa, AF.Exp,
                                bias=ccols[:, 17:18], scale=-1.0),
                                waits=[(PE, pev)] + (guard("fr") if ocb == 0 else []),
                                inc=SE)
                            setg(f"slot{slot}", (SE, SE.n))
                        else:
                            S.op("vector", lambda v, ocb=ocb, tw=tw, pa=pa: v.tensor_scalar(
                                lnt[:, 0:tw], pa, 0.0, None, ALU.max),
                                waits=[(PE, pev)] + (guard("fka") if ocb == 0 else []))
                            S.op("vector", lambda v, ocb=ocb, tw=tw: v.tensor_mul(
                                fka[:, ocb * TW: ocb * TW + tw],
                                lnt[:, 0:tw], lnt[:, 0:tw]), inc=VE)
                            setg(f"slot{slot}", (VE, VE.n))
                    if kind == "r":
                        sg = SE.n
                        tot = CB * TW
                        first_c = True
                        for r0 in range(0, tot, frtw):
                            cw = min(frtw, tot - r0)
                            last_c = (r0 + frtw >= tot)
                            S.op("scalar", lambda sc, r0=r0, cw=cw: sc.activation(
                                frt[:, 0:cw], fr[:, r0:r0 + cw], AF.Ln,
                                bias=onecol[:, 0:1], scale=1.0),
                                waits=[(SE, sg)] if first_c else ())
                            first_c = False
                            S.op("scalar", lambda sc, r0=r0, cw=cw: sc.activation(
                                fr[:, r0:r0 + cw], frt[:, 0:cw], AF.Exp,
                                bias=ccols[:, 17:18], scale=-1.0),
                                inc=SE if last_c else None)
                        se_r[0] = SE.n
                if kinds[-1][2] == "k":
                    setg("fx", (PE, PE.n))
                return vflush()

            def kv(ft, ve_ka):
                t0 = ft * TW
                tw = min(TW, ntok - t0)
                for ocb in range(CB):
                    buf = (2 * CB + ocb) % NBUF
                    S.op("sync" if buf % 2 == 0 else "gpsimd",
                         lambda s, ocb=ocb, buf=buf: s.dma_start(
                             out=fwc[:, buf * D:(buf + 1) * D], in_=wvf_hb.ap()[ocb]),
                         waits=wcg[buf], inc=WS[buf], inck=16)
                    wld = WS[buf].n
                    slot = ocb % 4
                    pa = ps(slot * 512, tw)
                    gw = [(VE, ve_ka), (WS[buf], wld), (SE, se_r[0])] + guard(f"slot{slot}")

                    def _mmv(t, buf=buf, tw=tw, pa=pa):
                        last = None
                        for cbb in range(CB):
                            last = t.matmul(
                                pa, fwc[:, buf * D + cbb * 128: buf * D + cbb * 128 + 128],
                                fka[:, cbb * TW: cbb * TW + tw],
                                start=(cbb == 0), stop=(cbb == CB - 1))
                        return last
                    S.op("tensor", _mmv, waits=gw, inc=PE)
                    wcg[buf] = [(PE, PE.n)]
                    pev = PE.n
                    S.op("vector", lambda v, ocb=ocb, tw=tw, pa=pa: v.scalar_tensor_tensor(
                        lnt[:, 0:tw], pa, 1.0, fr[:, ocb * TW: ocb * TW + tw],
                        ALU.mult, ALU.mult), waits=[(PE, pev)])
                    S.op("vector", lambda v, ocb=ocb, t0=t0, tw=tw: v.tensor_add(
                        dst_fn(ocb, t0, tw), dst_fn(ocb, t0, tw), lnt[:, 0:tw]), inc=VE)
                    setg(f"slot{slot}", (VE, VE.n))
                    if final_fn is not None:
                        vflush()
                        final_fn(ocb, t0, tw)
                setg("fr", (VE, VE.n))
                setg("fka", (PE, PE.n))

            vxA = frontA(0)
            vm = frontB(0, vxA)
            for ft in range(ntiles):
                rk(ft, vm, [(wr_hb, fxr, "r")])
                if ft + 1 < ntiles:
                    vxA = frontA(ft + 1)
                vk = rk(ft, vm, [(wkf_hb, fxk, "k")])
                if ft + 1 < ntiles:
                    vm = frontB(ft + 1, vxA)
                kv(ft, vk)

        resetB()
        fxn4 = sbufB("fxn4", [128, CB * 513], BF16)
        fxr4 = sbufB("fxr4", [128, CB * 512], BF16)
        fxk4 = sbufB("fxk4", [128, CB * 512], BF16)
        fka4 = sbufB("fka4", [128, CB * 512], BF16)
        fr4 = sbufB("fr4", [128, CB * 512], F32)
        fwc4 = sbufB("fwc4", [128, 2 * D], BF16)
        frt4 = nc.alloc_sbuf_tensor_at("frt4", [128, 512], F32,
                                       offset=OFFS["fxr4"])
        ffn_run(N, None,
                lambda cb, t0, tw: R[:, cb * N + t0: cb * N + t0 + tw],
                lambda ocb, t0, tw: R[:, ocb * N + t0: ocb * N + t0 + tw],
                None, fxn4, fxr4, fxk4, fka4, fr4, fwc4, frt4,
                TW=512, NBUF=2, frtw=512)
        S.fence(allvals())

        resetB()
        wbigQB = sbufB("wbigQB", [128, CB * D], BF16)
        wlat = sbufB("wlat", [128, CB * 256], BF16)
        wkv_b = sbufB("wkv_b", [128, 4 * 1024], BF16)
        lat = sbufB("lat", [128, 2 * 512], BF16)
        xn5 = sbufB("xn5", [128, CB * 512], BF16)
        evac5 = sbufB("evac5", [128, 4 * 512], BF16)
        S.op("sync", lambda s: s.dma_start(
            out=rfull_h.ap().rearrange("(cb p) t -> p cb t", p=128),
            in_=R.ap().rearrange("p (cb t) -> p cb t", cb=CB)), inc=LD, inck=16)
        rf_stored = LD.n

        wqb_ld = load_wbig(wbigQB, wq_b)
        S.op("gpsimd", lambda g: g.dma_start(
            out=wlat.ap().rearrange("p (cb o) -> p cb o", cb=CB),
            in_=wl_b.rearrange("(cb p) o -> p cb o", p=128)), inc=LG, inck=16)
        S.op("gpsimd", lambda g: g.dma_start(
            out=wkv_b.ap()[:, 0:2048].rearrange("p (cb o) -> p cb o", cb=2),
            in_=wk_b.rearrange("(cb p) o -> p cb o", p=128)), inc=LG, inck=16)
        S.op("gpsimd", lambda g: g.dma_start(
            out=wkv_b.ap()[:, 2048:4096].rearrange("p (cb o) -> p cb o", cb=2),
            in_=wv_b.rearrange("(cb p) o -> p cb o", p=128)), inc=LG, inck=16)
        wb_ld = LG.n

        for tt in range(8):
            t0 = tt * 512
            ve_xn = ln_stats_apply(
                512,
                lambda cb, t0=t0: R[:, cb * N + t0: cb * N + t0 + 512],
                lambda cb: xn5[:, cb * 512:(cb + 1) * 512],
                apply_waits=guard("xn"))

            def out_qb(ocb, c0, w, pa, pev, slot, t0=t0):
                ev = evac5[:, slot * 512: slot * 512 + w]
                S.op("vector", lambda v: v.tensor_scalar(
                    ev, pa, biases2[:, ocb:ocb + 1], None, ALU.add),
                    waits=[(PE, pev)] + guard(f"evac{slot}"), inc=VE)
                vv = VE.n
                S.op("sync", lambda s: s.dma_start(
                    out=qb_h.ap()[ocb][:, t0 + c0:t0 + c0 + w], in_=ev),
                    waits=[(VE, vv)], inc=LD, inck=16)
                setg(f"evac{slot}", (LD, LD.n))
                norm_track(ev, w, 16 + ocb, f"slot{slot}", pa)
            proj_cm(wbigQB, [(LG, wqb_ld)],
                    lambda cb, c0, w: xn5[:, cb * 512 + c0: cb * 512 + c0 + w],
                    ve_xn, 512, out_qb)

            for ocb in range(2):
                lslot = ocb % 2
                pa = ps(2048 + lslot * 512, 512)

                def _mml(t, ocb=ocb, pa=pa):
                    last = None
                    for cb in range(CB):
                        last = t.matmul(
                            pa, wlat[:, cb * 256 + ocb * 128: cb * 256 + ocb * 128 + 128],
                            xn5[:, cb * 512: cb * 512 + 512],
                            start=(cb == 0), stop=(cb == CB - 1))
                    return last
                S.op("tensor", _mml,
                     waits=[(VE, ve_xn), (LG, wb_ld)] + guard(f"vps{lslot}"), inc=PE)
                pev = PE.n
                S.op("vector", lambda v, ocb=ocb, pa=pa: v.tensor_scalar(
                    lat[:, ocb * 512:(ocb + 1) * 512], pa,
                    biases2[:, 24 + ocb:24 + ocb + 1], None, ALU.add),
                    waits=[(PE, pev)] + (guard("lat") if ocb == 0 else []),
                    inc=VE)
                setg(f"vps{lslot}", (VE, VE.n))
            ve_lat = vflush()

            def out_kb(ocb, c0, w, pa, pev, slot, t0=t0):
                ev = evac5[:, slot * 512: slot * 512 + w]
                S.op("vector", lambda v: v.tensor_scalar(
                    ev, pa, biases2[:, 8 + ocb:8 + ocb + 1], None, ALU.add),
                    waits=[(PE, pev)] + guard(f"evac{slot}"), inc=VE)
                vv = VE.n
                S.op("sync", lambda s: s.dma_start(
                    out=kb_h.ap()[ocb][:, t0 + c0:t0 + c0 + w], in_=ev),
                    waits=[(VE, vv)], inc=LD, inck=16)
                setg(f"evac{slot}", (LD, LD.n))
                norm_track(ev, w, 24 + ocb, f"slot{slot}", pa)
            for ocb in range(CB):
                slot = pslot[0]
                pslot[0] = (pslot[0] + 1) % 4
                pa = ps(slot * 512, 512)

                def _mmk(t, ocb=ocb, pa=pa):
                    last = None
                    for lb in range(2):
                        last = t.matmul(
                            pa, wkv_b[:, lb * 1024 + ocb * 128: lb * 1024 + ocb * 128 + 128],
                            lat[:, lb * 512: lb * 512 + 512],
                            start=(lb == 0), stop=(lb == 1))
                    return last
                S.op("tensor", _mmk,
                     waits=[(VE, ve_lat)] + guard(f"slot{slot}"), inc=PE)
                out_kb(ocb, 0, 512, pa, PE.n, slot)

            for tb in range(4):
                tof = t0 + tb * 128
                for vc in range(2):
                    pa = ps(2048 + vc * 512, 512)

                    def _mmvb(t, tb=tb, vc=vc, pa=pa):
                        last = None
                        for lb in range(2):
                            last = t.matmul(
                                pa, lat[:, lb * 512 + tb * 128: lb * 512 + tb * 128 + 128],
                                wkv_b[:, 2048 + lb * 1024 + vc * 512:
                                      2048 + lb * 1024 + vc * 512 + 512],
                                start=(lb == 0), stop=False)
                        return t.matmul(pa, ones_st[0:1, :], brow_b[0:1, vc * 512:vc * 512 + 512],
                                        start=False, stop=True)
                    S.op("tensor", _mmvb,
                         waits=[(VE, ve_lat)] + guard(f"vps{vc}"), inc=PE)
                    pev = PE.n
                    S.op("vector", lambda v, vc=vc, pa=pa: v.tensor_copy(
                        evac5[:, vc * 512:vc * 512 + 512], pa),
                        waits=[(PE, pev)] + guard(f"evac{vc}"), inc=VE)
                    setg(f"vps{vc}", (VE, VE.n))
                vv = VE.n
                S.op("sync", lambda s, tof=tof: s.dma_start(
                    out=vb_h.ap()[tof:tof + 128, :], in_=evac5[:, 0:1024]),
                    waits=[(VE, vv)], inc=LD, inck=16)
                setg("evac0", (LD, LD.n))
                setg("evac1", (LD, LD.n))
            setg("lat", (PE, PE.n))
            setg("xn", (PE, PE.n))

        emit_cbound(16, 24, 8, guard("vps0") + guard("vps1"))
        attn_gate[0] = VE.n
        qb_stored = LD.n
        S.op("gpsimd", lambda g: g.dma_start(
            out=qwin_h.ap(), in_=qb_h.ap()[:, :, bass.ds(dyn2[0], WIN)]),
            waits=[(LD, qb_stored)], inc=LG, inck=16)
        S.op("gpsimd", lambda g: g.dma_start(
            out=x2win_h.ap(), in_=rfull_h.ap()[:, bass.ds(dyn2[0], WIN)]),
            waits=[(LD, rf_stored)], inc=LG, inck=16)
        S.fence(allvals())

        resetB()
        kpbB = sbufB("kpbB", [128, N], BF16)
        vpbB = sbufB("vpbB", [128, 32 * 256], BF16)
        qtlB = sbufB("qtlB", [128, 1024], BF16)
        qtlB2 = sbufB("qtlB2", [128, 1024], BF16)
        expSB = sbufB("expSB", [128, 2 * 2048], BF16)
        o_unB = sbufB("o_unB", [128, N], BF16)
        rcpB = sbufB("rcpB", [128, 2048], F32)
        attn_phase(qwin_h, kb_h, vb_h, ccol=8, causal=False,
                   qwidths=[(0, 1024), (1024, 1024), (2048, 128)],
                   dynwin=True,
                   kpb=kpbB, vpb=vpbB, qtl=qtlB, expS=expSB, o_un=o_unB,
                   rcp=rcpB, qtl2=qtlB2)
        S.fence(allvals())

        resetB()
        wbigOB = sbufB("wbigOB", [128, CB * D], BF16)
        ot7 = sbufB("ot7", [128, CB * 512], BF16)
        x2w = sbufB("x2w", [128, CB * 512], F32)
        wob_ld = load_wbig(wbigOB, wo_b)
        for (t0, tw) in ((0, 512), (512, 512), (1024, 512), (1536, 512), (2048, 128)):
            for pr in range(NPAIR):
                S.op("sync", lambda s, pr=pr, t0=t0, tw=tw: s.dma_start(
                    out=ot7[:, pr * 512:pr * 512 + tw], in_=oh_h.ap()[pr][:, t0:t0 + tw]),
                    waits=guard("ot") if pr == 0 else [], inc=LD, inck=16)
            S.op("sync", lambda s, t0=t0, tw=tw: s.dma_start(
                out=x2w.ap().rearrange("p (cb t) -> p cb t", cb=CB)[:, :, 0:tw],
                in_=x2win_h.ap().rearrange("(cb p) t -> p cb t", p=128)
                [:, :, t0:t0 + tw]),
                waits=guard("x2w"), inc=LD, inck=16)
            old = LD.n

            def out_pb(ocb, c0, w, pa, pev, slot, t0=t0):
                S.op("vector", lambda v: v.scalar_tensor_tensor(
                    x2w[:, ocb * 512 + c0: ocb * 512 + c0 + w], pa,
                    biases2[:, 16 + ocb:16 + ocb + 1],
                    x2w[:, ocb * 512 + c0: ocb * 512 + c0 + w],
                    ALU.add, ALU.add), waits=[(PE, pev)], inc=VE)
                vv = VE.n
                setg(f"slot{slot}", (VE, vv))
                S.op("sync", lambda s: s.dma_start(
                    out=x3_h.ap()[ocb * 128:(ocb + 1) * 128, t0 + c0:t0 + c0 + w],
                    in_=x2w[:, ocb * 512 + c0: ocb * 512 + c0 + w]),
                    waits=[(VE, vv)], inc=LD, inck=16)
            proj_cm(wbigOB, [(LG, wob_ld), (LD, old)],
                    lambda cb, c0, w: ot7[:, cb * 512 + c0: cb * 512 + c0 + w],
                    0, tw, out_pb)
            setg("ot", (PE, PE.n))
            setg("x2w", (LD, LD.n))
        S.fence(allvals())

        resetB()
        fxn8 = sbufB("fxn8", [128, CB * 257], BF16)
        fxr8 = sbufB("fxr8", [128, CB * 256], BF16)
        fxk8 = sbufB("fxk8", [128, CB * 256], BF16)
        fka8 = sbufB("fka8", [128, CB * 256], BF16)
        fr8 = sbufB("fr8", [128, CB * 256], F32)
        fwc8 = sbufB("fwc8", [128, 4 * D], BF16)
        x3t = sbufB("x3t", [128, CB * 257], F32)
        x3t2 = sbufB("x3t2", [128, CB * 257], F32)
        frt8 = sbufB("frt8", [128, 1024], F32)

        x3bufs = [x3t, x3t2]
        x3sems = [QT, QT2]
        x3loadv = {}
        NT8 = ceil_div(WIN, 256)

        def _x3_load(ft):
            if ft >= NT8 or ft in x3loadv:
                return
            t0 = ft * 256
            tw = min(256, WIN - t0)
            buf = ft % 2
            S.op("sync", lambda s, t0=t0, tw=tw, buf=buf: s.dma_start(
                out=x3bufs[buf].ap().rearrange(
                    "p (cb t) -> p cb t", t=257)[:, :, 1:1 + tw],
                in_=x3_h.ap().rearrange("(cb p) t -> p cb t", p=128)[:, :, t0:t0 + tw]),
                waits=guard(f"x3t{buf}"), inc=x3sems[buf], inck=16)
            x3loadv[ft] = x3sems[buf].n

        def p8_pre(ft, t0, tw):
            _x3_load(ft)
            return [(x3sems[ft % 2], x3loadv[ft])]

        def p8_final(ocb, t0, tw):
            vv = VE.n
            buf = (t0 // 256) % 2
            S.op("sync", lambda s, ocb=ocb, t0=t0, tw=tw, buf=buf: s.dma_start(
                out=out_cm[ocb * 128:(ocb + 1) * 128, t0:t0 + tw],
                in_=x3bufs[buf][:, ocb * 257 + 1: ocb * 257 + 1 + tw]),
                waits=[(VE, vv)], inc=ST, inck=16)
            setg(f"x3t{buf}", (ST, ST.n))

        ffn_run(WIN, p8_pre,
                lambda cb, t0, tw: x3bufs[(t0 // 256) % 2][
                    :, cb * 257 + 1: cb * 257 + 1 + tw],
                lambda ocb, t0, tw: x3bufs[(t0 // 256) % 2][
                    :, ocb * 257 + 1: ocb * 257 + 1 + tw],
                p8_final, fxn8, fxr8, fxk8, fka8, fr8, fwc8, frt8)

        for name, (t, oap) in dbg.items():
            if name == "oh2":
                continue
            S.op("sync", lambda s, t=t, oap=oap: s.dma_start(out=oap, in_=t.ap()),
                 waits=allvals(), inc=ST, inck=16)
        for name, src in (("fr", fr8), ("fka", fka8), ("fxr", fxr8),
                          ("fxk", fxk8), ("fxn", fxn8), ("x3t", x3t)):
            if name in dbg_sb:
                S.op("sync", lambda s, src=src, oap=dbg_sb[name]: s.dma_start(
                    out=oap, in_=src.ap()), waits=allvals(), inc=ST, inck=16)

        S.op("sync", lambda s: None, waits=[(ST, ST.n), (LD, LD.n), (LG, LG.n)])

        with nc.Block() as block:
            S.emit(block)

    return nc



_NC_CACHE = {}


def _build_cached():
    if "nc" not in _NC_CACHE:
        _NC_CACHE["nc"] = build_program()
    return _NC_CACHE["nc"]


def _prep_inputs(inputs):
    f32 = np.float32
    x = np.asarray(inputs["L0_attn_a_encodings"], f32)
    attn_W = np.asarray(inputs["L0_attn_a_attn_W"], f32)
    attn_b = np.asarray(inputs["L0_attn_a_attn_b"], f32)
    out_W = np.asarray(inputs["L0_attn_a_out_W"], f32)
    out_b = np.asarray(inputs["L0_attn_a_out_b"], f32)

    wq = np.empty((D, D), f32); wk = np.empty((D, D), f32); wv = np.empty((D, D), f32)
    bq = np.empty(D, f32); bk = np.empty(D, f32); bv = np.empty(D, f32)
    for h in range(H):
        wq[:, h * HS:(h + 1) * HS] = attn_W[:, h * 192:h * 192 + 64]
        wk[:, h * HS:(h + 1) * HS] = attn_W[:, h * 192 + 64:h * 192 + 128]
        wv[:, h * HS:(h + 1) * HS] = attn_W[:, h * 192 + 128:h * 192 + 192]
        bq[h * HS:(h + 1) * HS] = attn_b[h * 192:h * 192 + 64]
        bk[h * HS:(h + 1) * HS] = attn_b[h * 192 + 64:h * 192 + 128]
        bv[h * HS:(h + 1) * HS] = attn_b[h * 192 + 128:h * 192 + 192]

    def cbcol(v):
        return np.ascontiguousarray(v.reshape(CB, 128).T)

    def cbcol2(v):
        return np.ascontiguousarray(v.reshape(2, 128).T)

    common = dict(
        wq_a=np.ascontiguousarray(wq), wk_a=np.ascontiguousarray(wk),
        wv_a=np.ascontiguousarray(wv),
        bq_a=cbcol(bq), bk_a=cbcol(bk),
        bv_a_row=np.ascontiguousarray(bv.reshape(1, D)),
        wo_a=np.ascontiguousarray(out_W), bo_a=cbcol(out_b),
        wr_f=np.ascontiguousarray(np.asarray(inputs["L0_ffn_W_r"], f32).T),
        wkf_f=np.ascontiguousarray(np.asarray(inputs["L0_ffn_W_k"], f32).T),
        wvf_f=np.ascontiguousarray(np.asarray(inputs["L0_ffn_W_v"], f32).T),
        mu_r=cbcol(np.asarray(inputs["L0_ffn_mu_r"], f32)),
        mu_k=cbcol(np.asarray(inputs["L0_ffn_mu_k"], f32)),
        wq_b=np.ascontiguousarray(np.asarray(inputs["L0_attn_b_Wq"], f32)),
        bq_b=cbcol(np.asarray(inputs["L0_attn_b_bq"], f32)),
        wl_b=np.ascontiguousarray(np.asarray(inputs["L0_attn_b_Wl"], f32)),
        bl_b=cbcol2(np.asarray(inputs["L0_attn_b_bl"], f32)),
        wk_b=np.ascontiguousarray(np.asarray(inputs["L0_attn_b_Wk"], f32)),
        bk_b=cbcol(np.asarray(inputs["L0_attn_b_bk"], f32)),
        wv_b=np.ascontiguousarray(np.asarray(inputs["L0_attn_b_Wv"], f32)),
        bv_b_row=np.ascontiguousarray(np.asarray(inputs["L0_attn_b_bv"], f32).reshape(1, D)),
        wo_b=np.ascontiguousarray(np.asarray(inputs["L0_attn_b_Wo"], f32)),
        bo_b=cbcol(np.asarray(inputs["L0_attn_b_bo"], f32)),
    )

    in_maps = []
    for c in range(8):
        m = dict(common)
        m["x0_cm"] = np.ascontiguousarray(x[c // 2].T)
        in_maps.append(m)
    return in_maps


def kernel(**inputs):
    nc = _build_cached()
    in_maps = _prep_inputs(inputs)
    res = run_bass_kernel_spmd(nc, in_maps, core_ids=list(range(8)))
    out = np.empty((4, N, D), np.float32)
    for c in range(8):
        b, h = c // 2, c % 2
        oc = np.asarray(res.results[c]["out_cm"])
        out[b, 2048 * h:2048 * (h + 1), :] = oc.T[128 * h:128 * h + 2048, :]
    return out


# version 24 (final)
# speedup vs baseline: 1.0214x; 1.0214x over previous
import numpy as np
import ml_dtypes
import concourse.bass as bass
import concourse.mybir as mybir
from concourse.bass_utils import run_bass_kernel_spmd
from contextlib import ExitStack

F32 = mybir.dt.float32
BF16 = mybir.dt.bfloat16
AF = mybir.ActivationFunctionType
ALU = mybir.AluOpType
AX = mybir.AxisListType.X

N = 4096
DIAG_FFN_FENCE = False
D = 1024
H = 16
HS = 64
CB = 8
NPAIR = 8
WIN = 2176
WOFF = 1920
LAT = 256
EPS = 1e-5
LN8 = float(np.log(8.0))


class Sem:
    def __init__(self, handle, name):
        self.h = handle
        self.name = name
        self.n = 0


class Sched:
    ENGINES = ("sync", "gpsimd", "tensor", "scalar", "vector")

    def __init__(self):
        self.ops = {e: [] for e in self.ENGINES}
        self.pending_fence = {e: [] for e in self.ENGINES}

    def fence(self, semvals):
        for e in self.ENGINES:
            self.pending_fence[e] = list(semvals)

    def op(self, eng, fn, waits=(), inc=None, inck=1):
        w = self.pending_fence[eng] + list(waits)
        self.pending_fence[eng] = []
        if inc is not None:
            inc.n += inck
        self.ops[eng].append((fn, w, inc, inck))
        return inc.n if inc is not None else None

    def emit(self, block):
        for eng_name in self.ENGINES:
            ops = self.ops[eng_name]
            if not ops:
                continue
            section = getattr(block, eng_name)

            def body(eng, ops=ops):
                for fn, waits, inc, inck in ops:
                    best = {}
                    for (sem, val) in waits:
                        if val > 0:
                            prev = best.get(sem.name)
                            best[sem.name] = (sem, max(val, prev[1]) if prev else val)
                    for sem, val in best.values():
                        eng.wait_ge(sem.h, val)
                    instr = fn(eng)
                    if inc is not None:
                        assert instr is not None
                        instr.then_inc(inc.h, inck)

            section(body)


def ceil_div(a, b):
    return (a + b - 1) // b


def build_program(debug_taps=()):
    nc = bass.Bass("TRN2", target_bir_lowering=False)

    def inp(name, shape):
        return nc.dram_tensor(name, shape, F32, kind="ExternalInput").ap()

    x0_cm = inp("x0_cm", [D, N])
    wq_a = inp("wq_a", [D, D]); wk_a = inp("wk_a", [D, D]); wv_a = inp("wv_a", [D, D])
    bq_a = inp("bq_a", [128, CB]); bk_a = inp("bk_a", [128, CB])
    bv_a_row = inp("bv_a_row", [1, D])
    wo_a = inp("wo_a", [D, D]); bo_a = inp("bo_a", [128, CB])
    wr_f = inp("wr_f", [D, D]); wkf_f = inp("wkf_f", [D, D]); wvf_f = inp("wvf_f", [D, D])
    mu_r = inp("mu_r", [128, CB]); mu_k = inp("mu_k", [128, CB])
    wq_b = inp("wq_b", [D, D]); bq_b = inp("bq_b", [128, CB])
    wl_b = inp("wl_b", [D, LAT]); bl_b = inp("bl_b", [128, 2])
    wk_b = inp("wk_b", [LAT, D]); bk_b = inp("bk_b", [128, CB])
    wv_b = inp("wv_b", [LAT, D]); bv_b_row = inp("bv_b_row", [1, D])
    wo_b = inp("wo_b", [D, D]); bo_b = inp("bo_b", [128, CB])

    tri_np = (np.arange(128)[:, None] <= np.arange(128)[None, :]).astype(np.float32)
    tri_c = nc.inline_tensor(tri_np, name="tri_c")
    sel_c = nc.inline_tensor(np.array([[0.0], [1.0]], np.float32), name="sel_c")

    out_cm = nc.dram_tensor("out_cm", [D, WIN], F32, kind="ExternalOutput").ap()

    qa_h = nc.dram_tensor("qa_h", [NPAIR, 128, N], BF16)
    ka_h = nc.dram_tensor("ka_h", [NPAIR, 128, N], BF16)
    va_h = nc.dram_tensor("va_h", [N, D], BF16)
    qb_h = nc.dram_tensor("qb_h", [NPAIR, 128, N], BF16)
    kb_h = nc.dram_tensor("kb_h", [NPAIR, 128, N], BF16)
    vb_h = nc.dram_tensor("vb_h", [N, D], BF16)
    oh_h = nc.dram_tensor("oh_h", [NPAIR, 128, N], BF16)
    rfull_h = nc.dram_tensor("rfull_h", [D, N], F32)
    qwin_h = nc.dram_tensor("qwin_h", [NPAIR, 128, WIN], BF16)
    x2win_h = nc.dram_tensor("x2win_h", [D, WIN], F32)
    x3_h = nc.dram_tensor("x3_h", [D, WIN], F32)
    wr_hb = nc.dram_tensor("wr_hb", [CB, 128, D], BF16)
    wkf_hb = nc.dram_tensor("wkf_hb", [CB, 128, D], BF16)
    wvf_hb = nc.dram_tensor("wvf_hb", [CB, 128, D], BF16)

    dbg = {}
    for name in debug_taps:
        t = {"qa": qa_h, "ka": ka_h, "va": va_h, "qb": qb_h, "kb": kb_h,
             "vb": vb_h, "oh": oh_h, "rfull": rfull_h, "x3": x3_h,
             "oh2": oh_h, "x1": None, "ccols": None, "nmax": None,
             "fr": None, "fka": None, "fxr": None, "fxk": None,
             "fxn": None, "x3t": None, "sc16": None}[name]
        if t is not None:
            dbg[name] = (t, nc.dram_tensor("dbg_" + name, list(t.shape), t.dtype,
                                           kind="ExternalOutput").ap())
    dbg_sb = {}
    for name, shape, dt in (("x1", [D, N], F32), ("ccols", [128, 18], F32),
                            ("nmax", [2, 32], F32), ("fr", [128, CB * 256], F32),
                            ("fka", [128, CB * 256], BF16), ("fxr", [128, CB * 256], BF16),
                            ("fxk", [128, CB * 256], BF16), ("fxn", [128, CB * 257], BF16),
                            ("x3t", [128, CB * 257], F32), ("sc16", [128, N], F32)):
        if name in debug_taps:
            dbg_sb[name] = nc.dram_tensor("dbg_" + name, shape, dt,
                                          kind="ExternalOutput").ap()

    with ExitStack() as ctx:
        SBUF_TOP = 229344
        cursor = {"A": 16384, "B": None}
        B_BASE = [0]

        OFFS = {}

        def _alloc(name, shape, dt, region):
            nbytes = int(np.prod(shape[1:])) * mybir.dt.size(dt)
            off = cursor[region]
            off = (off + 31) // 32 * 32
            cursor[region] = off + nbytes
            assert cursor[region] <= SBUF_TOP, (name, cursor[region])
            OFFS[name] = off
            return nc.alloc_sbuf_tensor_at(name, list(shape), dt, offset=off)

        def sbuf(name, shape, dt):
            return _alloc(name, shape, dt, "A")

        def sbufB(name, shape, dt):
            return _alloc(name, shape, dt, "B")

        def resetB():
            cursor["B"] = B_BASE[0]

        def mksem(name):
            return Sem(ctx.enter_context(nc.semaphore(name)), name)

        R = sbuf("R", [128, CB * N], F32)
        SC16 = sbuf("SC16", [128, N], F32)
        ones_st = sbuf("ones_st", [128, 128], BF16)
        trimask = sbuf("trimask", [128, 128], BF16)
        e2 = sbuf("e2", [128, 2], BF16)
        biases = sbuf("biases", [128, 24], F32)
        biases2 = sbuf("biases2", [128, 26], F32)
        mus = sbuf("mus", [128, 16], F32)
        brow_a = sbuf("brow_a", [1, D], BF16)
        brow_b = sbuf("brow_b", [1, D], BF16)
        sel_f = sbuf("sel_f", [2, 1], F32)
        ccols = sbuf("ccols", [128, 18], F32)
        onecol = sbuf("onecol", [128, 1], F32)
        nmax = sbuf("nmax", [2, 32], F32)
        cb_t = sbuf("cb_t", [2, 2 * NPAIR], F32)
        cb_ln8 = sbuf("cb_ln8", [2, 1], F32)
        ntmp = sbuf("ntmp", [2, 1], F32)
        nsq = sbuf("nsq", [128, 512], BF16)
        ones_f = sbuf("ones_f", [128, 128], F32)
        xb16 = sbuf("xb16", [128, 512], BF16)
        xsq = sbuf("xsq", [128, 512], BF16)
        xb16b = sbuf("xb16b", [128, 512], BF16)
        xsqb = sbuf("xsqb", [128, 512], BF16)
        B_BASE[0] = (cursor["A"] + 31) // 32 * 32
        resetB()

        class Carve:
            def __init__(self, base):
                self.base = base

            def __getitem__(self, key):
                rows, cols = key
                return SC16[rows, self.base + cols.start: self.base + cols.stop]

        lnw_a = Carve(0)
        lnw_b = Carve(1024)
        lnt = Carve(2048)
        st_negm = Carve(3072)
        st_var = Carve(3584)

        arena = ctx.enter_context(nc.psum_tensor("arena", [128, 4096], F32))

        LD = mksem("LD"); LG = mksem("LG"); PE = mksem("PE")
        SE = mksem("SE"); VE = mksem("VE"); ST = mksem("ST")
        WS = [mksem("WS0"), mksem("WS1"), mksem("WS2"), mksem("WS3")]
        QT = mksem("QT"); KV = mksem("KV"); OS = mksem("OS"); QT2 = mksem("QT2")
        ALLSEMS = [LD, LG, PE, SE, VE, ST, QT, KV, OS, QT2] + WS

        S = Sched()

        def allvals():
            return [(s, s.n) for s in ALLSEMS]

        def ps(c0, w):
            return arena[:, c0:c0 + w]

        G = {}

        def guard(name):
            return list(G.get(name, []))

        def setg(name, *svs):
            G[name] = [sv for sv in svs if sv[1] > 0]

        def vflush():
            return VE.n

        dyn = {}
        dyn2 = {}

        def _gp_init(g):
            pid = g.snap(nc.partition_id(), min_val=0, max_val=7)
            dyn2[0] = g.snap((pid % 2) * WOFF, min_val=0, max_val=WOFF)
            return None
        S.op("gpsimd", _gp_init)

        S.op("vector", lambda v: v.memset(ones_st[:], 1.0))
        S.op("vector", lambda v: v.memset(ones_f[:], 1.0))
        S.op("vector", lambda v: v.memset(e2[0:64, 0:1], 1.0))
        S.op("vector", lambda v: v.memset(e2[64:128, 0:1], 0.0))
        S.op("vector", lambda v: v.memset(e2[0:64, 1:2], 0.0))
        S.op("vector", lambda v: v.memset(e2[64:128, 1:2], 1.0))
        S.op("vector", lambda v: v.memset(ccols[:, 16:17], EPS))
        S.op("vector", lambda v: v.memset(ccols[:, 17:18], 0.0))
        S.op("vector", lambda v: v.memset(onecol[:], 1.0))
        S.op("vector", lambda v: v.memset(cb_ln8[:], -LN8))
        S.op("vector", lambda v: v.memset(nmax[:], 0.0), inc=VE)

        S.op("gpsimd", lambda g: g.dma_start(out=trimask[:], in_=tri_c.ap()), inc=LG, inck=16)
        S.op("gpsimd", lambda g: g.dma_start(out=brow_a[:], in_=bv_a_row), inc=LG, inck=16)
        S.op("gpsimd", lambda g: g.dma_start(out=brow_b[:], in_=bv_b_row), inc=LG, inck=16)
        S.op("gpsimd", lambda g: g.dma_start(out=sel_f[:], in_=sel_c.ap()), inc=LG, inck=16)
        for dst, src in ((biases[:, 0:8], bq_a), (biases[:, 8:16], bk_a),
                         (biases[:, 16:24], bo_a), (biases2[:, 0:8], bq_b),
                         (biases2[:, 8:16], bk_b), (biases2[:, 16:24], bo_b),
                         (biases2[:, 24:26], bl_b), (mus[:, 0:8], mu_r),
                         (mus[:, 8:16], mu_k)):
            S.op("sync", lambda s, dst=dst, src=src: s.dma_start(out=dst, in_=src),
                 inc=LD, inck=16)

        S.op("sync", lambda s: s.dma_start(
            out=R.ap().rearrange("p (cb t) -> p cb t", cb=CB),
            in_=x0_cm.rearrange("(cb p) t -> p cb t", p=128)), inc=LD, inck=16)

        wbigQ = sbufB("wbigQ", [128, CB * D], BF16)
        wbigK = sbufB("wbigK", [128, CB * D], BF16)
        wvh = sbufB("wvh", [128, CB * 512], BF16)
        xn1 = sbufB("xn1", [128, CB * 512], BF16)
        evac1 = sbufB("evac1", [128, 4 * 512], BF16)
        stash_g = []
        for (wsrc, wdst) in ((wr_f, wr_hb), (wkf_f, wkf_hb), (wvf_f, wvf_hb)):
            S.op("gpsimd", lambda g, wsrc=wsrc: g.dma_start(
                out=wbigQ.ap().rearrange("p (cb o) -> p cb o", cb=CB),
                in_=wsrc.rearrange("(cb p) o -> p cb o", p=128)),
                waits=list(stash_g), inc=LG, inck=16)
            ldv = LG.n

            def _store(s, wdst=wdst):
                for ocb in range(CB):
                    s.dma_start(
                        out=wdst.ap()[ocb].rearrange("p (cb j) -> p cb j", cb=CB),
                        in_=wbigQ.ap().rearrange("p (cb o) -> p cb o", cb=CB)
                        [:, :, ocb * 128:(ocb + 1) * 128]).then_inc(LD.h, 16)
                return None
            S.op("sync", _store, waits=[(LG, ldv)])
            LD.n += 16 * CB
            stash_g = [(LD, LD.n)]

        S.fence(allvals())

        def ln_stats_apply(tw, src_fn, dst_fn, src_waits=(), apply_waits=()):
            nsl = ceil_div(tw, 512)
            for si in range(nsl):
                c0 = si * 512
                w = min(512, tw - c0)
                for cb in range(CB):
                    xb = xb16 if cb % 2 == 0 else xb16b
                    xs = xsq if cb % 2 == 0 else xsqb
                    wts = (guard(f"xb16_{cb % 2}")
                           + (list(src_waits) if (cb == 0 and si == 0) else []))
                    S.op("vector", lambda v, cb=cb, c0=c0, w=w, xb=xb: v.tensor_copy(
                        xb[:, 0:w], src_fn(cb)[:, c0:c0 + w]), waits=wts)
                    S.op("vector", lambda v, cb=cb, w=w, xb=xb, xs=xs: v.tensor_mul(
                        xs[:, 0:w], xb[:, 0:w], xb[:, 0:w]), inc=VE)
                    vcb = VE.n

                    def _mmcb(t, cb=cb, w=w, xb=xb, xs=xs):
                        t.matmul(ps(3072, w), ones_st[:], xb[:, 0:w],
                                 start=(cb == 0), stop=(cb == CB - 1))
                        return t.matmul(ps(3584, w), ones_st[:], xs[:, 0:w],
                                        start=(cb == 0), stop=(cb == CB - 1))
                    S.op("tensor", _mmcb,
                         waits=[(VE, vcb)] + (guard("stats_ps") if cb == 0 else []),
                         inc=PE)
                    setg(f"xb16_{cb % 2}", (PE, PE.n))
                pev = PE.n
                S.op("vector", lambda v, w=w: v.tensor_scalar(
                    st_negm[:, 0:w], ps(3072, w), -1.0 / D, None, ALU.mult),
                    waits=[(PE, pev)] + guard("st"))
                S.op("vector", lambda v, w=w: v.tensor_scalar(
                    st_var[:, 0:w], ps(3584, w), 1.0 / D, None, ALU.mult))
                S.op("vector", lambda v, w=w: v.tensor_mul(
                    lnt[:, 0:w], st_negm[:, 0:w], st_negm[:, 0:w]))
                S.op("vector", lambda v, w=w: v.tensor_sub(
                    st_var[:, 0:w], st_var[:, 0:w], lnt[:, 0:w]), inc=VE)
                vstat = VE.n
                setg("stats_ps", (VE, vstat))
                S.op("scalar", lambda sc, w=w, c0=c0: sc.activation(
                    lnw_b[:, c0:c0 + w], st_var[:, 0:w], AF.Ln, bias=ccols[:, 16:17],
                    scale=1.0), waits=[(VE, vstat)] + guard("lnw"))
                S.op("scalar", lambda sc, w=w, c0=c0: sc.activation(
                    lnw_a[:, c0:c0 + w], lnw_b[:, c0:c0 + w], AF.Exp, bias=ccols[:, 17:18],
                    scale=-0.5), inc=SE)
                sev = SE.n
                S.op("vector", lambda v, w=w, c0=c0: v.tensor_mul(
                    lnw_b[:, c0:c0 + w], st_negm[:, 0:w], lnw_a[:, c0:c0 + w]),
                    waits=[(SE, sev)], inc=VE)
                setg("st", (VE, VE.n), (SE, sev))
            for cb in range(CB):
                wts = list(apply_waits) if cb == 0 else []
                S.op("vector", lambda v, cb=cb: v.tensor_mul(
                    lnt[:, 0:tw], src_fn(cb), lnw_a[:, 0:tw]), waits=wts)
                S.op("vector", lambda v, cb=cb: v.tensor_add(
                    dst_fn(cb), lnt[:, 0:tw], lnw_b[:, 0:tw]), inc=VE)
            setg("lnw", (VE, VE.n))
            return vflush()

        def norm_track(tile_ap, w, col, slot_key, pa2):
            S.op("vector", lambda v: v.tensor_mul(nsq[:, 0:w], tile_ap, tile_ap),
                 waits=guard("nsq"), inc=VE)
            vv = VE.n
            S.op("tensor", lambda t: t.matmul(pa2[0:2, 0:w], e2[:], nsq[:, 0:w],
                                              start=True, stop=True),
                 waits=[(VE, vv)], inc=PE)
            pv = PE.n
            setg("nsq", (PE, pv))
            S.op("vector", lambda v: v.reduce_max(ntmp[:, 0:1], pa2[0:2, 0:w], axis=AX),
                 waits=[(PE, pv)])
            S.op("vector", lambda v: v.tensor_max(nmax[:, col:col + 1],
                                                  nmax[:, col:col + 1], ntmp[0:2, 0:1]),
                 inc=VE)
            setg(slot_key, (VE, VE.n))

        def emit_cbound(qcol0, kcol0, outcol0, psw):
            vpre = VE.n
            S.op("scalar", lambda sc: sc.activation(
                cb_t[0:2, 0:NPAIR], nmax[0:2, qcol0:qcol0 + NPAIR], AF.Ln,
                bias=ccols[0:2, 17:18], scale=1.0), waits=[(VE, vpre)])
            S.op("scalar", lambda sc: sc.activation(
                cb_t[0:2, NPAIR:2 * NPAIR], nmax[0:2, kcol0:kcol0 + NPAIR], AF.Ln,
                bias=ccols[0:2, 17:18], scale=1.0), inc=SE)
            sv = SE.n
            S.op("vector", lambda v: v.tensor_add(
                cb_t[0:2, 0:NPAIR], cb_t[0:2, 0:NPAIR], cb_t[0:2, NPAIR:2 * NPAIR]),
                waits=[(SE, sv)], inc=VE)
            vv = VE.n
            S.op("scalar", lambda sc: sc.activation(
                cb_t[0:2, 0:NPAIR], cb_t[0:2, 0:NPAIR], AF.Exp,
                bias=cb_ln8[0:2, 0:1], scale=0.51), waits=[(VE, vv)], inc=SE)
            sv = SE.n
            S.op("tensor", lambda t: t.matmul(arena[0:1, 2048:2048 + NPAIR],
                                              sel_f[0:2, 0:1], cb_t[0:2, 0:NPAIR],
                                              start=True, stop=True),
                 waits=[(SE, sv)] + list(psw), inc=PE)
            pv = PE.n
            S.op("vector", lambda v: v.tensor_max(
                cb_t[0:1, 0:NPAIR], cb_t[0:1, 0:NPAIR], arena[0:1, 2048:2048 + NPAIR]),
                waits=[(PE, pv)], inc=VE)
            vv = VE.n

            def _bc(t):
                last = None
                for pr in range(NPAIR):
                    last = t.matmul(arena[:, 2048 + pr:2048 + pr + 1], ones_f[0:1, :],
                                    cb_t[0:1, pr:pr + 1], start=True, stop=True)
                return last
            S.op("tensor", _bc, waits=[(VE, vv)], inc=PE)
            pv = PE.n
            S.op("vector", lambda v: v.tensor_scalar(
                ccols[:, outcol0:outcol0 + NPAIR], arena[:, 2048:2048 + NPAIR],
                -1.0, -0.05, ALU.mult, ALU.add), waits=[(PE, pv)], inc=VE)

        pslot = [0]

        def proj_cm(wsb, w_waits, rhs_fn, rhs_ve, tw, out_fn, ncontr=CB):
            nsl = ceil_div(tw, 512)
            for ocb in range(CB):
                for si in range(nsl):
                    c0 = si * 512
                    w = min(512, tw - c0)
                    slot = pslot[0]
                    pslot[0] = (pslot[0] + 1) % 4
                    pa = ps(slot * 512, w)

                    def _mm(t, ocb=ocb, c0=c0, w=w, pa=pa):
                        last = None
                        for cb in range(ncontr):
                            last = t.matmul(
                                pa, wsb[:, cb * D + ocb * 128: cb * D + ocb * 128 + 128],
                                rhs_fn(cb, c0, w), start=(cb == 0), stop=(cb == ncontr - 1))
                        return last
                    wts = ([(VE, rhs_ve)] if rhs_ve else []) + list(w_waits) + guard(f"slot{slot}")
                    S.op("tensor", _mm, waits=wts, inc=PE)
                    out_fn(ocb, c0, w, pa, PE.n, slot)

        def load_wbig(dst, wsrc, waits=()):
            S.op("gpsimd", lambda g: g.dma_start(
                out=dst.ap().rearrange("p (cb o) -> p cb o", cb=CB),
                in_=wsrc.rearrange("(cb p) o -> p cb o", p=128)),
                waits=list(waits), inc=LG, inck=16)
            return LG.n

        wq_ld = load_wbig(wbigQ, wq_a, waits=stash_g)
        wk_ld = load_wbig(wbigK, wk_a)

        for tt in range(8):
            t0 = tt * 512
            ve_xn = ln_stats_apply(
                512,
                lambda cb, t0=t0: R[:, cb * N + t0: cb * N + t0 + 512],
                lambda cb: xn1[:, cb * 512:(cb + 1) * 512],
                apply_waits=guard("xn"))

            for (wsb, wld, hb, bcol0, ncol0) in ((wbigQ, wq_ld, qa_h, 0, 0),
                                                 (wbigK, wk_ld, ka_h, 8, 8)):
                def out_qk(ocb, c0, w, pa, pev, slot, hb=hb, bcol0=bcol0,
                           ncol0=ncol0, t0=t0):
                    ev = evac1[:, slot * 512: slot * 512 + w]
                    S.op("vector", lambda v: v.tensor_scalar(
                        ev, pa, biases[:, bcol0 + ocb:bcol0 + ocb + 1], None, ALU.add),
                        waits=[(PE, pev)] + guard(f"evac{slot}"), inc=VE)
                    vv = VE.n
                    S.op("sync", lambda s: s.dma_start(
                        out=hb.ap()[ocb][:, t0 + c0:t0 + c0 + w], in_=ev),
                        waits=[(VE, vv)], inc=LD, inck=16)
                    setg(f"evac{slot}", (LD, LD.n))
                    norm_track(ev, w, ncol0 + ocb, f"slot{slot}", pa)
                proj_cm(wsb, [(LG, wld)],
                        lambda cb, c0, w: xn1[:, cb * 512 + c0: cb * 512 + c0 + w],
                        ve_xn, 512, out_qk)

            for vc in range(2):
                S.op("gpsimd", lambda g, vc=vc: g.dma_start(
                    out=wvh.ap().rearrange("p (cb o) -> p cb o", cb=CB),
                    in_=wv_a[:, vc * 512:(vc + 1) * 512]
                    .rearrange("(cb p) o -> p cb o", p=128)),
                    waits=guard("wvh"), inc=LG, inck=16)
                wvl = LG.n
                for tb in range(4):
                    tof = t0 + tb * 128
                    pa = ps(2048 + (tb % 2) * 512, 512)

                    def _mmv(t, tb=tb, vc=vc, pa=pa):
                        last = None
                        for cb in range(CB):
                            last = t.matmul(
                                pa, xn1[:, cb * 512 + tb * 128: cb * 512 + tb * 128 + 128],
                                wvh[:, cb * 512: cb * 512 + 512],
                                start=(cb == 0), stop=False)
                        return t.matmul(pa, ones_st[0:1, :],
                                        brow_a[0:1, vc * 512:vc * 512 + 512],
                                        start=False, stop=True)
                    S.op("tensor", _mmv,
                         waits=[(VE, ve_xn), (LG, wvl)] + guard(f"vps{tb % 2}"), inc=PE)
                    pev = PE.n
                    setg("wvh", (PE, pev))
                    S.op("vector", lambda v, tb=tb, pa=pa: v.tensor_copy(
                        evac1[:, (tb % 2) * 512:(tb % 2) * 512 + 512], pa),
                        waits=[(PE, pev)] + guard(f"evac{tb % 2}"), inc=VE)
                    setg(f"vps{tb % 2}", (VE, VE.n))
                    vv = VE.n
                    S.op("sync", lambda s, tof=tof, tb=tb, vc=vc: s.dma_start(
                        out=va_h.ap()[tof:tof + 128, vc * 512:(vc + 1) * 512],
                        in_=evac1[:, (tb % 2) * 512:(tb % 2) * 512 + 512]),
                        waits=[(VE, vv)], inc=LD, inck=16)
                    setg(f"evac{tb % 2}", (LD, LD.n))
            setg("xn", (PE, PE.n))

        emit_cbound(0, 8, 0, guard("vps0") + guard("vps1"))
        attn_gate = [VE.n]
        S.fence(allvals())

        def attn_phase(q_hbm, k_hbm, v_hbm, ccol, causal, qwidths, dynwin,
                       kpb, vpb, qtl, expS, o_un, rcp, qtl2):
            qtot = sum(w for (_, w) in qwidths)

            def _emit_pvh(kt, c0, qw, pv_w, slot, half, first):
                def _pv(t, kt=kt, c0=c0, qw=qw, first=first, slot=slot, half=half):
                    last = None
                    cc = c0
                    while cc < qw:
                        w = min((512 - (cc % 512)) if (cc % 512) else 512, qw - cc)
                        last = t.matmul(
                            arena[:, 2048 + half * 1024 + cc:
                                  2048 + half * 1024 + cc + w],
                            vpb[:, kt * 256 + half * 128: kt * 256 + half * 128 + 128],
                            expS[:, slot * 1024 + cc: slot * 1024 + cc + w],
                            start=first, stop=False)
                        cc += w
                    return last
                if first:
                    pv_w = pv_w + guard("pv_ps")
                S.op("tensor", _pv, waits=pv_w, inc=PE)
                setg(f"expS{slot}", (PE, PE.n))
            S.op("vector", lambda v: v.memset(
                vpb.ap().rearrange("p (tb q) -> p tb q", tb=32)[:, :, 64:192], 1.0),
                inc=VE)
            setg("vpb", (VE, VE.n))
            for pr in range(NPAIR):
                S.op("sync", lambda s, pr=pr: s.dma_start(
                    out=kpb[:], in_=k_hbm.ap()[pr]),
                    waits=guard("kpb"), inc=KV, inck=16)
                S.op("gpsimd", lambda g, pr=pr: g.dma_start(
                    out=vpb.ap().rearrange("p (tb q) -> p tb q", tb=32)[:, :, 0:64],
                    in_=v_hbm.ap()[:, pr * 128:pr * 128 + 64]
                    .rearrange("(tb p) c -> p tb c", p=128)),
                    waits=guard("vpb"), inc=KV, inck=16)
                S.op("gpsimd", lambda g, pr=pr: g.dma_start(
                    out=vpb.ap().rearrange("p (tb q) -> p tb q", tb=32)[:, :, 192:256],
                    in_=v_hbm.ap()[:, pr * 128 + 64:pr * 128 + 128]
                    .rearrange("(tb p) c -> p tb c", p=128)), inc=KV, inck=16)
                kvld = KV.n

                ve_collect = VE.n
                qbufs = [qtl, qtl2]
                qsems = [QT, QT2]
                qloadv = {}

                def _q_load(ti):
                    if ti >= len(qwidths) or ti in qloadv:
                        return
                    qloc, qw = qwidths[ti]
                    b = ti % 2
                    S.op("sync", lambda s, pr=pr, qloc=qloc, qw=qw, b=b: s.dma_start(
                        out=qbufs[b][:, 0:qw], in_=q_hbm.ap()[pr][:, qloc:qloc + qw]),
                        waits=guard(f"qtl{b}"), inc=qsems[b], inck=16)
                    qloadv[ti] = qsems[b].n

                for ti, (qloc, qw) in enumerate(qwidths):
                    _q_load(ti)
                    _q_load(ti + 1)
                    qtlb = qbufs[ti % 2]
                    qld = qloadv[ti]
                    qsem = qsems[ti % 2]

                    nkt = 32 if not causal else (qloc + qw) // 128
                    pend = None
                    for i in range(nkt):
                        kt = i
                        k0 = kt * 128
                        diag = causal and (k0 >= qloc)
                        c0 = (k0 - qloc) if diag else 0
                        pvs = []
                        for half in range(2):
                            slot = (2 * i + half) % 4

                            def _smmh(t, kt=kt, c0=c0, qw=qw, half=half, qtlb=qtlb):
                                b0 = half * 64
                                tp = (0, 0) if half == 0 else (64, 0)
                                last = None
                                cc = c0
                                while cc < qw:
                                    w = min((512 - (cc % 512)) if (cc % 512) else 512,
                                            qw - cc)
                                    last = t.matmul(
                                        arena[:, half * 1024 + cc: half * 1024 + cc + w],
                                        kpb[b0:b0 + 64, kt * 128:(kt + 1) * 128],
                                        qtlb[b0:b0 + 64, cc:cc + w],
                                        start=True, stop=True, tile_position=tp)
                                    cc += w
                                return last
                            sw = [(KV, kvld), (qsem, qld)] + guard(f"score{half}")
                            if i == 0 and half == 0:
                                sw += ([(VE, attn_gate[0])] + guard("pv_ps")
                                       + guard("score0") + guard("score1"))
                            S.op("tensor", _smmh, waits=sw, inc=PE)
                            pvs.append([kt, c0, qw, PE.n, slot, half])

                        for half in range(2):
                            kt_, c0_, qw_, pev, slot, _h = pvs[half]
                            S.op("scalar",
                                 lambda sc, pr=pr, c0=c0, qw=qw, slot=slot, half=half:
                                 sc.activation(
                                     expS[:, slot * 1024 + c0: slot * 1024 + qw],
                                     arena[:, half * 1024 + c0: half * 1024 + qw],
                                     AF.Exp, bias=ccols[:, ccol + pr:ccol + pr + 1],
                                     scale=0.125),
                                 waits=[(PE, pev)] + guard(f"expS{slot}"), inc=SE)
                            sev = SE.n
                            setg(f"score{half}", (SE, sev))
                            pv_w = [(SE, sev)]
                            if diag:
                                S.op("vector", lambda v, c0=c0, slot=slot: v.tensor_mul(
                                    expS[:, slot * 1024 + c0: slot * 1024 + c0 + 128],
                                    expS[:, slot * 1024 + c0: slot * 1024 + c0 + 128],
                                    trimask[:]), waits=[(SE, sev)], inc=VE)
                                pv_w.append((VE, VE.n))
                            pvs[half][3] = pv_w

                        if i == 0:
                            for (kt_, c0_, qw_, pv_w, slot, half) in pvs:
                                _emit_pvh(kt_, c0_, qw_, pv_w, slot, half, True)
                            pend = None
                        else:
                            if pend is not None:
                                for (kt_, c0_, qw_, pv_w, slot, half) in pend:
                                    _emit_pvh(kt_, c0_, qw_, pv_w, slot, half, False)
                            pend = pvs
                    if pend is not None:
                        for (kt_, c0_, qw_, pv_w, slot, half) in pend:
                            _emit_pvh(kt_, c0_, qw_, pv_w, slot, half, False)
                    pv_done = PE.n

                    S.op("vector", lambda v, qloc=qloc, qw=qw: v.tensor_copy(
                        SC16[64:128, qloc:qloc + qw], arena[64:128, 2048:2048 + qw]),
                        waits=[(PE, pv_done)] + guard("SC16") + guard("o_un"))
                    S.op("vector", lambda v, qloc=qloc, qw=qw: v.tensor_copy(
                        SC16[0:64, qloc:qloc + qw], arena[0:64, 3072:3072 + qw]))
                    S.op("vector", lambda v, qloc=qloc, qw=qw: v.tensor_copy(
                        o_un[0:64, qloc:qloc + qw], arena[0:64, 2048:2048 + qw]))
                    S.op("vector", lambda v, qloc=qloc, qw=qw: v.tensor_copy(
                        o_un[64:128, qloc:qloc + qw], arena[64:128, 3072:3072 + qw]), inc=VE)
                    ve_collect = vflush()
                    setg("pv_ps", (VE, ve_collect))
                    setg(f"qtl{ti % 2}", (PE, pv_done))

                first = True
                for r0 in range(0, qtot, 2048):
                    rw = min(2048, qtot - r0)
                    S.op("scalar", lambda sc, r0=r0, rw=rw: sc.activation(
                        rcp[:, 0:rw], SC16[:, r0:r0 + rw], AF.Ln, bias=ccols[:, 17:18],
                        scale=1.0), waits=[(VE, ve_collect)] if first else ())
                    first = False
                    S.op("scalar", lambda sc, r0=r0, rw=rw: sc.activation(
                        SC16[:, r0:r0 + rw], rcp[:, 0:rw], AF.Exp, bias=ccols[:, 17:18],
                        scale=-1.0), inc=SE)
                se_norm = SE.n

                def _realign(t, qtot=qtot):
                    last = None
                    for cc in range(0, qtot, 512):
                        w = min(512, qtot - cc)
                        last = t.matmul(arena[0:64, cc:cc + w], ones_f[64:65, 0:64],
                                        SC16[64:65, cc:cc + w], start=True, stop=True)
                        last = t.matmul(arena[64:128, cc:cc + w], ones_f[0:1, 0:64],
                                        SC16[0:1, cc:cc + w], start=True, stop=True)
                    return last
                S.op("tensor", _realign,
                     waits=([(SE, se_norm)] + guard("score0") + guard("score1")
                            + guard("pv_ps")), inc=PE)
                pe_re = PE.n
                S.op("vector", lambda v, qtot=qtot: v.tensor_mul(
                    o_un[:, 0:qtot], o_un[:, 0:qtot], arena[:, 0:qtot]),
                    waits=[(PE, pe_re)], inc=VE)
                ve_norm = vflush()
                S.op("sync", lambda s, pr=pr, qtot=qtot: s.dma_start(
                    out=oh_h.ap()[pr][:, 0:qtot], in_=o_un[:, 0:qtot]),
                    waits=[(VE, ve_norm)], inc=OS, inck=16)
                setg("o_un", (OS, OS.n))
                setg("SC16", (PE, pe_re))
                setg("score0", (VE, ve_norm))
                setg("score1", (VE, ve_norm))
                setg("pv_ps", (VE, ve_norm))
                setg("kpb", (PE, pv_done))
                setg("vpb", (PE, pv_done))

        resetB()
        kpbA = sbufB("kpbA", [128, N], BF16)
        vpbA = sbufB("vpbA", [128, 32 * 256], BF16)
        qtlA = sbufB("qtlA", [128, 1024], BF16)
        qtlA2 = sbufB("qtlA2", [128, 1024], BF16)
        expSA = sbufB("expSA", [128, 2 * 2048], BF16)
        o_unA = sbufB("o_unA", [128, N], BF16)
        rcpA = sbufB("rcpA", [128, 2048], F32)
        attn_phase(qa_h, ka_h, va_h, ccol=0, causal=True,
                   qwidths=[(0, 1024), (1024, 1024), (2048, 1024), (3072, 1024)],
                   dynwin=False,
                   kpb=kpbA, vpb=vpbA, qtl=qtlA, expS=expSA, o_un=o_unA,
                   rcp=rcpA, qtl2=qtlA2)
        S.fence(allvals())
        if "oh2" in dbg:
            S.op("sync", lambda s: s.dma_start(out=dbg["oh2"][1], in_=oh_h.ap()),
                 inc=ST, inck=16)
        if "ccols" in dbg_sb:
            S.op("sync", lambda s: s.dma_start(out=dbg_sb["ccols"], in_=ccols[:]),
                 inc=ST, inck=16)
        if "nmax" in dbg_sb:
            S.op("sync", lambda s: s.dma_start(out=dbg_sb["nmax"], in_=nmax[:]),
                 inc=ST, inck=16)
        S.fence(allvals())

        resetB()
        wbigO = sbufB("wbigO", [128, CB * D], BF16)
        ot3 = sbufB("ot3", [128, CB * 1024], BF16)
        woa_ld = load_wbig(wbigO, wo_a)
        for tt in range(4):
            t0 = tt * 1024
            for pr in range(NPAIR):
                S.op("sync", lambda s, pr=pr, t0=t0: s.dma_start(
                    out=ot3[:, pr * 1024:(pr + 1) * 1024], in_=oh_h.ap()[pr][:, t0:t0 + 1024]),
                    waits=guard("ot") if pr == 0 else [], inc=LD, inck=16)
            old = LD.n

            def out_po(ocb, c0, w, pa, pev, slot, t0=t0):
                S.op("vector", lambda v: v.scalar_tensor_tensor(
                    R[:, ocb * N + t0 + c0: ocb * N + t0 + c0 + w], pa,
                    biases[:, 16 + ocb:16 + ocb + 1],
                    R[:, ocb * N + t0 + c0: ocb * N + t0 + c0 + w],
                    ALU.add, ALU.add), waits=[(PE, pev)], inc=VE)
                setg(f"slot{slot}", (VE, VE.n))
            proj_cm(wbigO, [(LG, woa_ld), (LD, old)],
                    lambda cb, c0, w: ot3[:, cb * 1024 + c0: cb * 1024 + c0 + w],
                    0, 1024, out_po)
            setg("ot", (PE, PE.n))
        S.fence(allvals())
        if "x1" in dbg_sb:
            S.op("sync", lambda s: s.dma_start(
                out=dbg_sb["x1"].rearrange("(cb p) t -> p cb t", p=128),
                in_=R.ap().rearrange("p (cb t) -> p cb t", cb=CB)), inc=ST, inck=16)
            S.fence(allvals())

        def ffn_run(ntok, pre_tile_fn, src_fn, dst_fn, final_fn,
                    fxn, fxr, fxk, fka, fr, fwc, frt,
                    TW=256, NBUF=4, frtw=1024):
            wcg = [[] for _ in range(NBUF)]
            se_r = [0]
            tw_prev = [0]
            ntiles = ceil_div(ntok, TW)

            def frontA(ft):
                t0 = ft * TW
                tw = min(TW, ntok - t0)
                extra = (pre_tile_fn(ft, t0, tw) or []) if pre_tile_fn else []
                if ft == 0:
                    for cb in range(CB):
                        S.op("vector", lambda v, cb=cb: v.memset(
                            fxn[:, cb * (TW + 1):cb * (TW + 1) + 1], 0.0))
                else:
                    for cb in range(CB):
                        S.op("vector", lambda v, cb=cb, tp=tw_prev[0]: v.tensor_copy(
                            fxn[:, cb * (TW + 1):cb * (TW + 1) + 1],
                            fxn[:, cb * (TW + 1) + tp:cb * (TW + 1) + tp + 1]))
                ve_xn = ln_stats_apply(
                    tw,
                    lambda cb, t0=t0, tw=tw: src_fn(cb, t0, tw),
                    lambda cb, tw=tw: fxn[:, cb * (TW + 1) + 1: cb * (TW + 1) + 1 + tw],
                    src_waits=extra, apply_waits=guard("fx"))
                tw_prev[0] = tw
                return ve_xn

            def frontB(ft, ve_xn):
                t0 = ft * TW
                tw = min(TW, ntok - t0)
                for cb in range(CB):
                    S.op("vector", lambda v, cb=cb, tw=tw: v.tensor_sub(
                        lnt[:, 0:tw], fxn[:, cb * (TW + 1) + 1: cb * (TW + 1) + 1 + tw],
                        fxn[:, cb * (TW + 1): cb * (TW + 1) + tw]),
                        waits=([(VE, ve_xn)] + guard("fx") + guard("fka")) if cb == 0 else [])
                    S.op("vector", lambda v, cb=cb, tw=tw: v.scalar_tensor_tensor(
                        fxr[:, cb * TW: cb * TW + tw], lnt[:, 0:tw],
                        mus[:, cb:cb + 1], fxn[:, cb * (TW + 1): cb * (TW + 1) + tw],
                        ALU.mult, ALU.add))
                    S.op("vector", lambda v, cb=cb, tw=tw: v.scalar_tensor_tensor(
                        fxk[:, cb * TW: cb * TW + tw], lnt[:, 0:tw],
                        mus[:, 8 + cb:8 + cb + 1], fxn[:, cb * (TW + 1): cb * (TW + 1) + tw],
                        ALU.mult, ALU.add), inc=VE)
                return vflush()

            def rk(ft, ve_mix, kinds):
                t0 = ft * TW
                tw = min(TW, ntok - t0)
                for (wh, xsb, kind) in kinds:
                    for ocb in range(CB):
                        buf = (ocb + (0 if kind == "r" else CB)) % NBUF
                        S.op("sync" if buf % 2 == 0 else "gpsimd",
                             lambda s, wh=wh, ocb=ocb, buf=buf: s.dma_start(
                                 out=fwc[:, buf * D:(buf + 1) * D], in_=wh.ap()[ocb]),
                             waits=wcg[buf], inc=WS[buf], inck=16)
                        wld = WS[buf].n
                        slot = ocb % 4
                        pa = ps(slot * 512, tw)
                        gw = [(VE, ve_mix), (WS[buf], wld)] + guard(f"slot{slot}")

                        def _mmf(t, xsb=xsb, buf=buf, tw=tw, pa=pa):
                            last = None
                            for cbb in range(CB):
                                last = t.matmul(
                                    pa, fwc[:, buf * D + cbb * 128: buf * D + cbb * 128 + 128],
                                    xsb[:, cbb * TW: cbb * TW + tw],
                                    start=(cbb == 0), stop=(cbb == CB - 1))
                            return last
                        S.op("tensor", _mmf, waits=gw, inc=PE)
                        wcg[buf] = [(PE, PE.n)]
                        pev = PE.n
                        if kind == "r":
                            S.op("scalar", lambda sc, ocb=ocb, tw=tw, pa=pa: sc.activation(
                                fr[:, ocb * TW: ocb * TW + tw], pa, AF.Exp,
                                bias=ccols[:, 17:18], scale=-1.0),
                                waits=[(PE, pev)] + (guard("fr") if ocb == 0 else []),
                                inc=SE)
                            setg(f"slot{slot}", (SE, SE.n))
                        else:
                            S.op("vector", lambda v, ocb=ocb, tw=tw, pa=pa: v.tensor_scalar(
                                lnt[:, 0:tw], pa, 0.0, None, ALU.max),
                                waits=[(PE, pev)] + (guard("fka") if ocb == 0 else []))
                            S.op("vector", lambda v, ocb=ocb, tw=tw: v.tensor_mul(
                                fka[:, ocb * TW: ocb * TW + tw],
                                lnt[:, 0:tw], lnt[:, 0:tw]), inc=VE)
                            setg(f"slot{slot}", (VE, VE.n))
                    if kind == "r":
                        sg = SE.n
                        tot = CB * TW
                        first_c = True
                        for r0 in range(0, tot, frtw):
                            cw = min(frtw, tot - r0)
                            last_c = (r0 + frtw >= tot)
                            S.op("scalar", lambda sc, r0=r0, cw=cw: sc.activation(
                                frt[:, 0:cw], fr[:, r0:r0 + cw], AF.Ln,
                                bias=onecol[:, 0:1], scale=1.0),
                                waits=[(SE, sg)] if first_c else ())
                            first_c = False
                            S.op("scalar", lambda sc, r0=r0, cw=cw: sc.activation(
                                fr[:, r0:r0 + cw], frt[:, 0:cw], AF.Exp,
                                bias=ccols[:, 17:18], scale=-1.0),
                                inc=SE if last_c else None)
                        se_r[0] = SE.n
                if kinds[-1][2] == "k":
                    setg("fx", (PE, PE.n))
                return vflush()

            def kv(ft, ve_ka):
                t0 = ft * TW
                tw = min(TW, ntok - t0)
                for ocb in range(CB):
                    buf = (2 * CB + ocb) % NBUF
                    S.op("sync" if buf % 2 == 0 else "gpsimd",
                         lambda s, ocb=ocb, buf=buf: s.dma_start(
                             out=fwc[:, buf * D:(buf + 1) * D], in_=wvf_hb.ap()[ocb]),
                         waits=wcg[buf], inc=WS[buf], inck=16)
                    wld = WS[buf].n
                    slot = ocb % 4
                    pa = ps(slot * 512, tw)
                    gw = [(VE, ve_ka), (WS[buf], wld), (SE, se_r[0])] + guard(f"slot{slot}")

                    def _mmv(t, buf=buf, tw=tw, pa=pa):
                        last = None
                        for cbb in range(CB):
                            last = t.matmul(
                                pa, fwc[:, buf * D + cbb * 128: buf * D + cbb * 128 + 128],
                                fka[:, cbb * TW: cbb * TW + tw],
                                start=(cbb == 0), stop=(cbb == CB - 1))
                        return last
                    S.op("tensor", _mmv, waits=gw, inc=PE)
                    wcg[buf] = [(PE, PE.n)]
                    pev = PE.n
                    S.op("vector", lambda v, ocb=ocb, tw=tw, pa=pa: v.scalar_tensor_tensor(
                        lnt[:, 0:tw], pa, 1.0, fr[:, ocb * TW: ocb * TW + tw],
                        ALU.mult, ALU.mult), waits=[(PE, pev)])
                    S.op("vector", lambda v, ocb=ocb, t0=t0, tw=tw: v.tensor_add(
                        dst_fn(ocb, t0, tw), dst_fn(ocb, t0, tw), lnt[:, 0:tw]), inc=VE)
                    setg(f"slot{slot}", (VE, VE.n))
                    if final_fn is not None:
                        vflush()
                        final_fn(ocb, t0, tw)
                setg("fr", (VE, VE.n))
                setg("fka", (PE, PE.n))

            vxA = frontA(0)
            vm = frontB(0, vxA)
            for ft in range(ntiles):
                rk(ft, vm, [(wr_hb, fxr, "r")])
                if ft + 1 < ntiles:
                    vxA = frontA(ft + 1)
                vk = rk(ft, vm, [(wkf_hb, fxk, "k")])
                if ft + 1 < ntiles:
                    vm = frontB(ft + 1, vxA)
                kv(ft, vk)

        resetB()
        fxn4 = sbufB("fxn4", [128, CB * 513], BF16)
        fxr4 = sbufB("fxr4", [128, CB * 512], BF16)
        fxk4 = sbufB("fxk4", [128, CB * 512], BF16)
        fka4 = sbufB("fka4", [128, CB * 512], BF16)
        fr4 = sbufB("fr4", [128, CB * 512], F32)
        fwc4 = sbufB("fwc4", [128, 2 * D], BF16)
        frt4 = nc.alloc_sbuf_tensor_at("frt4", [128, 512], F32,
                                       offset=OFFS["fxr4"])
        ffn_run(N, None,
                lambda cb, t0, tw: R[:, cb * N + t0: cb * N + t0 + tw],
                lambda ocb, t0, tw: R[:, ocb * N + t0: ocb * N + t0 + tw],
                None, fxn4, fxr4, fxk4, fka4, fr4, fwc4, frt4,
                TW=512, NBUF=2, frtw=512)
        S.fence(allvals())

        resetB()
        wbigQB = sbufB("wbigQB", [128, CB * D], BF16)
        wlat = sbufB("wlat", [128, CB * 256], BF16)
        wkv_b = sbufB("wkv_b", [128, 4 * 1024], BF16)
        lat = sbufB("lat", [128, 2 * 512], BF16)
        xn5 = sbufB("xn5", [128, CB * 512], BF16)
        evac5 = sbufB("evac5", [128, 4 * 512], BF16)
        S.op("sync", lambda s: s.dma_start(
            out=rfull_h.ap().rearrange("(cb p) t -> p cb t", p=128),
            in_=R.ap().rearrange("p (cb t) -> p cb t", cb=CB)), inc=LD, inck=16)
        rf_stored = LD.n

        wqb_ld = load_wbig(wbigQB, wq_b)
        S.op("gpsimd", lambda g: g.dma_start(
            out=wlat.ap().rearrange("p (cb o) -> p cb o", cb=CB),
            in_=wl_b.rearrange("(cb p) o -> p cb o", p=128)), inc=LG, inck=16)
        S.op("gpsimd", lambda g: g.dma_start(
            out=wkv_b.ap()[:, 0:2048].rearrange("p (cb o) -> p cb o", cb=2),
            in_=wk_b.rearrange("(cb p) o -> p cb o", p=128)), inc=LG, inck=16)
        S.op("gpsimd", lambda g: g.dma_start(
            out=wkv_b.ap()[:, 2048:4096].rearrange("p (cb o) -> p cb o", cb=2),
            in_=wv_b.rearrange("(cb p) o -> p cb o", p=128)), inc=LG, inck=16)
        wb_ld = LG.n

        for tt in range(8):
            t0 = tt * 512
            ve_xn = ln_stats_apply(
                512,
                lambda cb, t0=t0: R[:, cb * N + t0: cb * N + t0 + 512],
                lambda cb: xn5[:, cb * 512:(cb + 1) * 512],
                apply_waits=guard("xn"))

            def out_qb(ocb, c0, w, pa, pev, slot, t0=t0):
                ev = evac5[:, slot * 512: slot * 512 + w]
                S.op("vector", lambda v: v.tensor_scalar(
                    ev, pa, biases2[:, ocb:ocb + 1], None, ALU.add),
                    waits=[(PE, pev)] + guard(f"evac{slot}"), inc=VE)
                vv = VE.n
                S.op("sync", lambda s: s.dma_start(
                    out=qb_h.ap()[ocb][:, t0 + c0:t0 + c0 + w], in_=ev),
                    waits=[(VE, vv)], inc=LD, inck=16)
                setg(f"evac{slot}", (LD, LD.n))
                norm_track(ev, w, 16 + ocb, f"slot{slot}", pa)
            proj_cm(wbigQB, [(LG, wqb_ld)],
                    lambda cb, c0, w: xn5[:, cb * 512 + c0: cb * 512 + c0 + w],
                    ve_xn, 512, out_qb)

            for ocb in range(2):
                lslot = ocb % 2
                pa = ps(2048 + lslot * 512, 512)

                def _mml(t, ocb=ocb, pa=pa):
                    last = None
                    for cb in range(CB):
                        last = t.matmul(
                            pa, wlat[:, cb * 256 + ocb * 128: cb * 256 + ocb * 128 + 128],
                            xn5[:, cb * 512: cb * 512 + 512],
                            start=(cb == 0), stop=(cb == CB - 1))
                    return last
                S.op("tensor", _mml,
                     waits=[(VE, ve_xn), (LG, wb_ld)] + guard(f"vps{lslot}"), inc=PE)
                pev = PE.n
                S.op("vector", lambda v, ocb=ocb, pa=pa: v.tensor_scalar(
                    lat[:, ocb * 512:(ocb + 1) * 512], pa,
                    biases2[:, 24 + ocb:24 + ocb + 1], None, ALU.add),
                    waits=[(PE, pev)] + (guard("lat") if ocb == 0 else []),
                    inc=VE)
                setg(f"vps{lslot}", (VE, VE.n))
            ve_lat = vflush()

            def out_kb(ocb, c0, w, pa, pev, slot, t0=t0):
                ev = evac5[:, slot * 512: slot * 512 + w]
                S.op("vector", lambda v: v.tensor_scalar(
                    ev, pa, biases2[:, 8 + ocb:8 + ocb + 1], None, ALU.add),
                    waits=[(PE, pev)] + guard(f"evac{slot}"), inc=VE)
                vv = VE.n
                S.op("sync", lambda s: s.dma_start(
                    out=kb_h.ap()[ocb][:, t0 + c0:t0 + c0 + w], in_=ev),
                    waits=[(VE, vv)], inc=LD, inck=16)
                setg(f"evac{slot}", (LD, LD.n))
                norm_track(ev, w, 24 + ocb, f"slot{slot}", pa)
            for ocb in range(CB):
                slot = pslot[0]
                pslot[0] = (pslot[0] + 1) % 4
                pa = ps(slot * 512, 512)

                def _mmk(t, ocb=ocb, pa=pa):
                    last = None
                    for lb in range(2):
                        last = t.matmul(
                            pa, wkv_b[:, lb * 1024 + ocb * 128: lb * 1024 + ocb * 128 + 128],
                            lat[:, lb * 512: lb * 512 + 512],
                            start=(lb == 0), stop=(lb == 1))
                    return last
                S.op("tensor", _mmk,
                     waits=[(VE, ve_lat)] + guard(f"slot{slot}"), inc=PE)
                out_kb(ocb, 0, 512, pa, PE.n, slot)

            for tb in range(4):
                tof = t0 + tb * 128
                for vc in range(2):
                    pa = ps(2048 + vc * 512, 512)

                    def _mmvb(t, tb=tb, vc=vc, pa=pa):
                        last = None
                        for lb in range(2):
                            last = t.matmul(
                                pa, lat[:, lb * 512 + tb * 128: lb * 512 + tb * 128 + 128],
                                wkv_b[:, 2048 + lb * 1024 + vc * 512:
                                      2048 + lb * 1024 + vc * 512 + 512],
                                start=(lb == 0), stop=False)
                        return t.matmul(pa, ones_st[0:1, :], brow_b[0:1, vc * 512:vc * 512 + 512],
                                        start=False, stop=True)
                    S.op("tensor", _mmvb,
                         waits=[(VE, ve_lat)] + guard(f"vps{vc}"), inc=PE)
                    pev = PE.n
                    S.op("vector", lambda v, vc=vc, pa=pa: v.tensor_copy(
                        evac5[:, vc * 512:vc * 512 + 512], pa),
                        waits=[(PE, pev)] + guard(f"evac{vc}"), inc=VE)
                    setg(f"vps{vc}", (VE, VE.n))
                vv = VE.n
                S.op("sync", lambda s, tof=tof: s.dma_start(
                    out=vb_h.ap()[tof:tof + 128, :], in_=evac5[:, 0:1024]),
                    waits=[(VE, vv)], inc=LD, inck=16)
                setg("evac0", (LD, LD.n))
                setg("evac1", (LD, LD.n))
            setg("lat", (PE, PE.n))
            setg("xn", (PE, PE.n))

        emit_cbound(16, 24, 8, guard("vps0") + guard("vps1"))
        attn_gate[0] = VE.n
        qb_stored = LD.n
        S.op("gpsimd", lambda g: g.dma_start(
            out=qwin_h.ap(), in_=qb_h.ap()[:, :, bass.ds(dyn2[0], WIN)]),
            waits=[(LD, qb_stored)], inc=LG, inck=16)
        S.op("gpsimd", lambda g: g.dma_start(
            out=x2win_h.ap(), in_=rfull_h.ap()[:, bass.ds(dyn2[0], WIN)]),
            waits=[(LD, rf_stored)], inc=LG, inck=16)
        S.fence(allvals())

        resetB()
        kpbB = sbufB("kpbB", [128, N], BF16)
        vpbB = sbufB("vpbB", [128, 32 * 256], BF16)
        qtlB = sbufB("qtlB", [128, 1024], BF16)
        qtlB2 = sbufB("qtlB2", [128, 1024], BF16)
        expSB = sbufB("expSB", [128, 2 * 2048], BF16)
        o_unB = sbufB("o_unB", [128, N], BF16)
        rcpB = sbufB("rcpB", [128, 2048], F32)
        attn_phase(qwin_h, kb_h, vb_h, ccol=8, causal=False,
                   qwidths=[(0, 1024), (1024, 1024), (2048, 128)],
                   dynwin=True,
                   kpb=kpbB, vpb=vpbB, qtl=qtlB, expS=expSB, o_un=o_unB,
                   rcp=rcpB, qtl2=qtlB2)
        S.fence(allvals())

        resetB()
        wbigOB = sbufB("wbigOB", [128, CB * D], BF16)
        ot7 = sbufB("ot7", [128, CB * 512], BF16)
        x2w = sbufB("x2w", [128, CB * 512], F32)
        wob_ld = load_wbig(wbigOB, wo_b)
        for (t0, tw) in ((0, 512), (512, 512), (1024, 512), (1536, 512), (2048, 128)):
            for pr in range(NPAIR):
                S.op("sync", lambda s, pr=pr, t0=t0, tw=tw: s.dma_start(
                    out=ot7[:, pr * 512:pr * 512 + tw], in_=oh_h.ap()[pr][:, t0:t0 + tw]),
                    waits=guard("ot") if pr == 0 else [], inc=LD, inck=16)
            S.op("sync", lambda s, t0=t0, tw=tw: s.dma_start(
                out=x2w.ap().rearrange("p (cb t) -> p cb t", cb=CB)[:, :, 0:tw],
                in_=x2win_h.ap().rearrange("(cb p) t -> p cb t", p=128)
                [:, :, t0:t0 + tw]),
                waits=guard("x2w"), inc=LD, inck=16)
            old = LD.n

            def out_pb(ocb, c0, w, pa, pev, slot, t0=t0):
                S.op("vector", lambda v: v.scalar_tensor_tensor(
                    x2w[:, ocb * 512 + c0: ocb * 512 + c0 + w], pa,
                    biases2[:, 16 + ocb:16 + ocb + 1],
                    x2w[:, ocb * 512 + c0: ocb * 512 + c0 + w],
                    ALU.add, ALU.add), waits=[(PE, pev)], inc=VE)
                vv = VE.n
                setg(f"slot{slot}", (VE, vv))
                S.op("sync", lambda s: s.dma_start(
                    out=x3_h.ap()[ocb * 128:(ocb + 1) * 128, t0 + c0:t0 + c0 + w],
                    in_=x2w[:, ocb * 512 + c0: ocb * 512 + c0 + w]),
                    waits=[(VE, vv)], inc=LD, inck=16)
            proj_cm(wbigOB, [(LG, wob_ld), (LD, old)],
                    lambda cb, c0, w: ot7[:, cb * 512 + c0: cb * 512 + c0 + w],
                    0, tw, out_pb)
            setg("ot", (PE, PE.n))
            setg("x2w", (LD, LD.n))
        S.fence(allvals())

        resetB()
        fxn8 = sbufB("fxn8", [128, CB * 257], BF16)
        fxr8 = sbufB("fxr8", [128, CB * 256], BF16)
        fxk8 = sbufB("fxk8", [128, CB * 256], BF16)
        fka8 = sbufB("fka8", [128, CB * 256], BF16)
        fr8 = sbufB("fr8", [128, CB * 256], F32)
        fwc8 = sbufB("fwc8", [128, 4 * D], BF16)
        x3t = sbufB("x3t", [128, CB * 257], F32)
        x3t2 = sbufB("x3t2", [128, CB * 257], F32)
        frt8 = sbufB("frt8", [128, 1024], F32)

        x3bufs = [x3t, x3t2]
        x3sems = [QT, QT2]
        x3loadv = {}
        NT8 = ceil_div(WIN, 256)

        def _x3_load(ft):
            if ft >= NT8 or ft in x3loadv:
                return
            t0 = ft * 256
            tw = min(256, WIN - t0)
            buf = ft % 2
            S.op("sync", lambda s, t0=t0, tw=tw, buf=buf: s.dma_start(
                out=x3bufs[buf].ap().rearrange(
                    "p (cb t) -> p cb t", t=257)[:, :, 1:1 + tw],
                in_=x3_h.ap().rearrange("(cb p) t -> p cb t", p=128)[:, :, t0:t0 + tw]),
                waits=guard(f"x3t{buf}"), inc=x3sems[buf], inck=16)
            x3loadv[ft] = x3sems[buf].n

        def p8_pre(ft, t0, tw):
            _x3_load(ft)
            return [(x3sems[ft % 2], x3loadv[ft])]

        def p8_final(ocb, t0, tw):
            vv = VE.n
            buf = (t0 // 256) % 2
            S.op("sync", lambda s, ocb=ocb, t0=t0, tw=tw, buf=buf: s.dma_start(
                out=out_cm[ocb * 128:(ocb + 1) * 128, t0:t0 + tw],
                in_=x3bufs[buf][:, ocb * 257 + 1: ocb * 257 + 1 + tw]),
                waits=[(VE, vv)], inc=ST, inck=16)
            setg(f"x3t{buf}", (ST, ST.n))

        ffn_run(WIN, p8_pre,
                lambda cb, t0, tw: x3bufs[(t0 // 256) % 2][
                    :, cb * 257 + 1: cb * 257 + 1 + tw],
                lambda ocb, t0, tw: x3bufs[(t0 // 256) % 2][
                    :, ocb * 257 + 1: ocb * 257 + 1 + tw],
                p8_final, fxn8, fxr8, fxk8, fka8, fr8, fwc8, frt8)

        for name, (t, oap) in dbg.items():
            if name == "oh2":
                continue
            S.op("sync", lambda s, t=t, oap=oap: s.dma_start(out=oap, in_=t.ap()),
                 waits=allvals(), inc=ST, inck=16)
        for name, src in (("fr", fr8), ("fka", fka8), ("fxr", fxr8),
                          ("fxk", fxk8), ("fxn", fxn8), ("x3t", x3t)):
            if name in dbg_sb:
                S.op("sync", lambda s, src=src, oap=dbg_sb[name]: s.dma_start(
                    out=oap, in_=src.ap()), waits=allvals(), inc=ST, inck=16)

        S.op("sync", lambda s: None, waits=[(ST, ST.n), (LD, LD.n), (LG, LG.n)])

        with nc.Block() as block:
            S.emit(block)

    return nc



_NC_CACHE = {}


def _build_cached():
    if "nc" not in _NC_CACHE:
        _NC_CACHE["nc"] = build_program()
    return _NC_CACHE["nc"]


def _prep_inputs(inputs):
    f32 = np.float32
    x = np.asarray(inputs["L0_attn_a_encodings"], f32)
    attn_W = np.asarray(inputs["L0_attn_a_attn_W"], f32)
    attn_b = np.asarray(inputs["L0_attn_a_attn_b"], f32)
    out_W = np.asarray(inputs["L0_attn_a_out_W"], f32)
    out_b = np.asarray(inputs["L0_attn_a_out_b"], f32)

    wq = np.empty((D, D), f32); wk = np.empty((D, D), f32); wv = np.empty((D, D), f32)
    bq = np.empty(D, f32); bk = np.empty(D, f32); bv = np.empty(D, f32)
    for h in range(H):
        wq[:, h * HS:(h + 1) * HS] = attn_W[:, h * 192:h * 192 + 64]
        wk[:, h * HS:(h + 1) * HS] = attn_W[:, h * 192 + 64:h * 192 + 128]
        wv[:, h * HS:(h + 1) * HS] = attn_W[:, h * 192 + 128:h * 192 + 192]
        bq[h * HS:(h + 1) * HS] = attn_b[h * 192:h * 192 + 64]
        bk[h * HS:(h + 1) * HS] = attn_b[h * 192 + 64:h * 192 + 128]
        bv[h * HS:(h + 1) * HS] = attn_b[h * 192 + 128:h * 192 + 192]

    def cbcol(v):
        return np.ascontiguousarray(v.reshape(CB, 128).T)

    def cbcol2(v):
        return np.ascontiguousarray(v.reshape(2, 128).T)

    common = dict(
        wq_a=np.ascontiguousarray(wq), wk_a=np.ascontiguousarray(wk),
        wv_a=np.ascontiguousarray(wv),
        bq_a=cbcol(bq), bk_a=cbcol(bk),
        bv_a_row=np.ascontiguousarray(bv.reshape(1, D)),
        wo_a=np.ascontiguousarray(out_W), bo_a=cbcol(out_b),
        wr_f=np.ascontiguousarray(np.asarray(inputs["L0_ffn_W_r"], f32).T),
        wkf_f=np.ascontiguousarray(np.asarray(inputs["L0_ffn_W_k"], f32).T),
        wvf_f=np.ascontiguousarray(np.asarray(inputs["L0_ffn_W_v"], f32).T),
        mu_r=cbcol(np.asarray(inputs["L0_ffn_mu_r"], f32)),
        mu_k=cbcol(np.asarray(inputs["L0_ffn_mu_k"], f32)),
        wq_b=np.ascontiguousarray(np.asarray(inputs["L0_attn_b_Wq"], f32)),
        bq_b=cbcol(np.asarray(inputs["L0_attn_b_bq"], f32)),
        wl_b=np.ascontiguousarray(np.asarray(inputs["L0_attn_b_Wl"], f32)),
        bl_b=cbcol2(np.asarray(inputs["L0_attn_b_bl"], f32)),
        wk_b=np.ascontiguousarray(np.asarray(inputs["L0_attn_b_Wk"], f32)),
        bk_b=cbcol(np.asarray(inputs["L0_attn_b_bk"], f32)),
        wv_b=np.ascontiguousarray(np.asarray(inputs["L0_attn_b_Wv"], f32)),
        bv_b_row=np.ascontiguousarray(np.asarray(inputs["L0_attn_b_bv"], f32).reshape(1, D)),
        wo_b=np.ascontiguousarray(np.asarray(inputs["L0_attn_b_Wo"], f32)),
        bo_b=cbcol(np.asarray(inputs["L0_attn_b_bo"], f32)),
    )

    in_maps = []
    for c in range(8):
        m = dict(common)
        m["x0_cm"] = np.ascontiguousarray(x[c // 2].T)
        in_maps.append(m)
    return in_maps


def kernel(**inputs):
    nc = _build_cached()
    in_maps = _prep_inputs(inputs)
    res = run_bass_kernel_spmd(nc, in_maps, core_ids=list(range(8)))
    out = np.empty((4, N, D), np.float32)
    for c in range(8):
        b, h = c // 2, c % 2
        oc = np.asarray(res.results[c]["out_cm"])
        out[b, 2048 * h:2048 * (h + 1), :] = oc.T[128 * h:128 * h + 2048, :]
    return out
